# version 13
# baseline (speedup 1.0000x reference)
import contextlib
import math

import numpy as np

import concourse.bass as bass
import concourse.mybir as mybir
from concourse.bass_utils import run_bass_kernel_spmd

F32 = mybir.dt.float32
BF16 = mybir.dt.bfloat16
AF = mybir.ActivationFunctionType
ALU = mybir.AluOpType
AX = mybir.AxisListType

T = 4096
D = 2048
H = 16
DH = 128
LAT = 1024
HID = 1408
NE = 8
NKT = D // 128
NLT = LAT // 128
NJ = HID // 128
CH = 512
SCALE = 1.0 / math.sqrt(DH)
EXP_SHIFT = -12.0

f32 = np.float32

_TAPS = set()
_PHASES = {"B1", "B2", "B3", "C", "D", "E0", "E"}
_DT_SZ = {F32: 4, BF16: 2}


class Prog:
    ENGS = ("T", "V", "S", "G", "Y")
    ENGSEM = {"T": "semT", "V": "semV", "S": "semS"}
    HAZ = []

    def __init__(self, nc, es):
        self.nc = nc
        self.es = es
        self.ops = {e: [] for e in self.ENGS}
        self.cnt = {}
        self.handles = {}
        self.last_w = {}
        self.readers = {}
        self.wm = {e: {} for e in self.ENGS}
        self.pending = {e: [] for e in self.ENGS}
        self.pending_keys = {}

    def sem(self, name):
        if name not in self.handles:
            self.handles[name] = self.es.enter_context(self.nc.semaphore(name))
            self.cnt[name] = 0
        return self.handles[name]

    def _wait(self, eng, sem, val):
        if self.wm[eng].get(sem, 0) >= val:
            return
        if sem.startswith("d_") and val < self.cnt.get(sem, 0):
            Prog.HAZ.append((eng, sem, val, self.cnt[sem]))
        self.wm[eng][sem] = val
        self.ops[eng].append(("w", sem, val))

    def op(self, eng, fn, reads=(), writes=(), sem=None, inc=None, war=(), tag=""):
        need = {}
        for k in reads:
            if not k:
                continue
            if k in self.pending_keys:
                assert self.pending_keys[k] == eng, f"{eng} reads pending key {k} ({tag})"
            for s, v in self.last_w.get(k, {}).items():
                need[s] = max(need.get(s, 0), v)
        for k in tuple(writes) + tuple(war):
            for s, v in self.last_w.get(k, {}).items():
                need[s] = max(need.get(s, 0), v)
            for s, v in self.readers.get(k, {}).items():
                need[s] = max(need.get(s, 0), v)
        for s, v in sorted(need.items()):
            self._wait(eng, s, v)
        if sem is None:
            sem = self.ENGSEM.get(eng)
        if inc is None:
            inc = 1 if eng in self.ENGSEM else 16
        self.ops[eng].append(("i", fn, sem if inc else None, inc, tag))
        if inc:
            self.sem(sem)
            self.cnt[sem] += inc
            sv, cv = sem, self.cnt[sem]
            items = self.pending[eng] + [(tuple(reads), tuple(writes))]
            self.pending[eng] = []
            for rds, wrs in items:
                for k in wrs:
                    self.pending_keys.pop(k, None)
                    lw = self.last_w.setdefault(k, {})
                    lw[sv] = max(lw.get(sv, 0), cv)
                    self.readers[k] = {}
                for k in rds:
                    if not k:
                        continue
                    rd = self.readers.setdefault(k, {})
                    rd[sv] = max(rd.get(sv, 0), cv)
        else:
            self.pending[eng].append((tuple(reads), tuple(writes)))
            for k in writes:
                self.pending_keys[k] = eng

    def pace(self, eng, sem):
        self._wait(eng, sem, self.cnt.get(sem, 0))

    def dma(self, eng, fn, reads=(), writes=(), slot_sem=None, war=(), tag=""):
        self.op(eng, fn, reads, writes, sem=slot_sem, inc=16, war=war, tag=tag)

    def dma_group(self, eng, fns, reads=(), dst_key=None, slot_sem=None, tag=""):
        n = len(fns)
        for i, fn in enumerate(fns):
            self.dma(eng, fn, reads=reads if i == 0 else (),
                     writes=(dst_key,) if i == n - 1 else (),
                     war=(dst_key,) if i == 0 else (),
                     slot_sem=slot_sem, tag=tag)

    def barrier(self):
        for e in self.ENGS:
            assert not self.pending[e], f"pending ops at barrier on {e}"
        own = {v: k for k, v in self.ENGSEM.items()}
        for e in self.ENGS:
            for s in sorted(self.cnt):
                if own.get(s) == e:
                    continue
                if self.cnt[s] > 0:
                    self._wait(e, s, self.cnt[s])

    def replay(self, block):
        def mk(e):
            def run(eng):
                for entry in self.ops[e]:
                    if entry[0] == "w":
                        eng.wait_ge(self.handles[entry[1]], entry[2])
                    else:
                        _, fn, sem, inc, _tag = entry
                        ins = fn(eng)
                        if sem is not None:
                            ins.then_inc(self.handles[sem], inc)
            return run
        block.tensor(mk("T"))
        block.vector(mk("V"))
        block.scalar(mk("S"))
        block.gpsimd(mk("G"))
        block.sync(mk("Y"))


def _rope_perm():
    perm = np.empty(D, dtype=np.int64)
    for h in range(H):
        for j in range(64):
            perm[h * 128 + j] = h * 128 + 2 * j
            perm[h * 128 + 64 + j] = h * 128 + 2 * j + 1
    return perm


def _lhsT_tiles(w):
    K, M = w.shape
    return np.ascontiguousarray(w.reshape(K // 128, 128, M // 128, 128).transpose(1, 0, 2, 3))


def _rhs_tiles(w):
    K, N = w.shape
    return np.ascontiguousarray(w.reshape(K // 128, 128, N).transpose(1, 0, 2))


def _panel_lhsT(w):
    return np.ascontiguousarray(w.reshape(NKT, 128, NJ, 128).transpose(2, 1, 0, 3))


def _panel_rhs(w):
    return np.ascontiguousarray(w.reshape(NJ, 128, D))


def build_program():
    NCH = T // CH
    NTT = T // 128
    QTR = T // 4
    nc = bass.Bass("TRN2", target_bir_lowering=False)

    def dram(name, shape, dtype, kind="Internal"):
        k = "ExternalOutput" if name in _TAPS and kind == "Internal" else kind
        return nc.dram_tensor(name, list(shape), dtype, kind=k).ap()

    d_x = dram("x", (T, D), F32, "ExternalInput")
    d_wq = dram("wq_r", (128, NKT, D), F32, "ExternalInput")
    d_wdn = dram("wdn_l", (128, NKT, NLT, 128), F32, "ExternalInput")
    d_wuk = dram("wupk_r", (128, NLT, D), F32, "ExternalInput")
    d_wuv = dram("wupv_r", (128, NLT, D), F32, "ExternalInput")
    d_wo = dram("wo_r", (128, NKT, D), F32, "ExternalInput")
    d_gate = dram("gate_r", (128, NKT, NE), F32, "ExternalInput")
    d_w1 = dram("w1p", (NE, NJ, 128, NKT, 128), F32, "ExternalInput")
    d_w3 = dram("w3p", (NE, NJ, 128, NKT, 128), F32, "ExternalInput")
    d_w2 = dram("w2p", (NE, NJ, 128, D), F32, "ExternalInput")
    d_sw1 = dram("sw1p", (NJ, 128, NKT, 128), F32, "ExternalInput")
    d_sw3 = dram("sw3p", (NJ, 128, NKT, 128), F32, "ExternalInput")
    d_sw2 = dram("sw2p", (NJ, 128, D), F32, "ExternalInput")
    d_eb = dram("ebias", (1, NE), F32, "ExternalInput")
    d_cos = dram("cosb", (128, NTT, 64), F32, "ExternalInput")
    d_sin = dram("sinb", (128, NTT, 64), F32, "ExternalInput")
    d_mask = dram("masks", (4, 128, CH), F32, "ExternalInput")
    d_onec = dram("onec", (128, 1), F32, "ExternalInput")
    d_oner = dram("oner", (1, 128), F32, "ExternalInput")
    d_idf = dram("identf", (128, 128), F32, "ExternalInput")
    d_oh = dram("onehot", (16, NKT * 128), F32, "ExternalInput")

    d_out = dram("out", (T, D), F32, "ExternalOutput")

    s_xbf = dram("xbf", (T, D), BF16)
    s_ckvT = dram("ckvTs", (LAT, T), BF16)
    s_qrow = dram("qrow", (T, D), BF16)
    s_krow = dram("krow", (T, D), BF16)
    s_vrow = dram("vrow", (T, D), BF16)
    s_aout = dram("aoutT", (D, T), BF16)
    s_rcs = dram("rcs", (H, T), F32)
    s_xtbf = dram("xtbf", (T, D), BF16)
    s_xtf = dram("xtf", (T, D), F32)
    b_w1 = dram("w1b", (NE, NJ, 128, NKT * 128), BF16)
    b_w3 = dram("w3b", (NE, NJ, 128, NKT * 128), BF16)
    b_w2 = dram("w2b", (NE, NJ, 128, D), BF16)
    b_sw1 = dram("sw1b", (NJ, 128, NKT * 128), BF16)
    b_sw3 = dram("sw3b", (NJ, 128, NKT * 128), BF16)
    b_sw2 = dram("sw2b", (NJ, 128, D), BF16)

    es = contextlib.ExitStack()
    with es:
        p = Prog(nc, es)

        alloc_state = {"off": (int(nc.sbuf_base) + 127) // 128 * 128}

        def salloc(name, shape, dtype, off=None):
            nbytes = int(np.prod(shape[1:])) * _DT_SZ[dtype]
            if off is None:
                off = alloc_state["off"]
                alloc_state["off"] = off + ((nbytes + 127) // 128) * 128
            return nc.alloc_sbuf_tensor_at(name, list(shape), dtype, offset=off)

        sb_onec = salloc("sb_onec", [128, 1], BF16)
        sb_oner = salloc("sb_oner", [1, 128], BF16)
        sb_idf = salloc("sb_idf", [128, 128], F32)
        sb_gate = salloc("sb_gate", [128, NKT * NE], BF16)
        sb_eb = salloc("sb_eb", [1, NE], BF16)
        sb_mall = salloc("sb_mall", [128, NTT * NE], F32)
        MARK_SMALL = alloc_state["off"]
        sb_mask = salloc("sb_mask", [128, 4 * CH], BF16)
        MARK_MASK = alloc_state["off"]
        sb_cos = salloc("sb_cos", [128, NTT * 64], F32)
        sb_sin = salloc("sb_sin", [128, NTT * 64], F32)
        MARK_SIN = alloc_state["off"]
        SBUF_TOP = nc.sbuf_top

        class Arena:
            def __init__(self, start):
                self.off = start
                self.idx = 0

            def al(self, name, shape, dtype):
                nbytes = int(np.prod(shape[1:])) * _DT_SZ[dtype]
                off = self.off
                self.off = off + ((nbytes + 127) // 128) * 128
                assert self.off <= SBUF_TOP, f"SBUF overflow {name}: {self.off} > {SBUF_TOP}"
                self.idx += 1
                return salloc(f"{name}_{self.idx}", list(shape), dtype, off=off)

        PS = [es.enter_context(nc.psum_tensor(f"psbank{i}", [128, CH], F32)) for i in range(8)]
        WQ_OFF = MARK_SIN + 81920
        WO_OFF = 162304
        sb_wq_g = salloc("sb_wq_g", [128, NKT * D], BF16, off=WQ_OFF)
        sb_wo_g = salloc("sb_wo_g", [128, NKT * D], BF16, off=WO_OFF)

        p.dma("Y", lambda e: e.dma_start(sb_cos[:, :], d_cos.rearrange("p t j -> p (t j)")),
              (), ("cos",), "d_cos")
        p.dma("Y", lambda e: e.dma_start(sb_sin[:, :], d_sin.rearrange("p t j -> p (t j)")),
              (), ("sin",), "d_sin")
        p.dma("Y", lambda e: e.dma_start(sb_idf[:, :], d_idf), (), ("idf",), "d_idf")
        p.dma("G", lambda e: e.dma_start(sb_onec[:, :], d_onec), (), ("onec",), "d_onec")
        p.dma("G", lambda e: e.dma_start(sb_oner[:, :], d_oner), (), ("oner",), "d_oner")
        p.dma_group("G", [
            (lambda dlt: (lambda e: e.dma_start(sb_mask[:, dlt * CH:(dlt + 1) * CH], d_mask[dlt])))(dlt)
            for dlt in range(4)], (), "mask", "d_mask")
        p.dma("G", lambda e: e.dma_start(sb_gate[:, :], d_gate.rearrange("p k e -> p (k e)")),
              (), ("gate",), "d_gate")
        p.dma("G", lambda e: e.dma_start(sb_eb[:, :], d_eb), (), ("ebias",), "d_ebias")

        for qq in range(4):
            p.dma("G", lambda e, qq=qq: e.dma_start(
                s_xbf[qq * QTR:(qq + 1) * QTR, :], d_x[qq * QTR:(qq + 1) * QTR, :]),
                (), (f"xbf{qq}",), f"d_xbf{qq}", tag="xcast")

        def xbf_keys(c):
            return tuple(f"xbf{q}" for q in range((c * CH) // QTR, ((c + 1) * CH - 1) // QTR + 1))

        def load_xT(dst, dst_key, src, c, src_keys):
            fns = [
                (lambda kt: (lambda e: e.dma_start_transpose(
                    dst[:, kt * CH:(kt + 1) * CH],
                    src[c * CH:(c + 1) * CH, kt * 128:(kt + 1) * 128])))(kt)
                for kt in range(NKT)]
            p.dma_group("Y", fns, src_keys, dst_key, "d_" + dst_key, tag="xT")

        def rope_and_store(key_qf, sb_qf, sb_ta, sb_tb, sb_qb, key_qb, tt, dst, dst_key, st_sem):
            qv = sb_qf[:, :].rearrange("p (h t j) -> p h t j", h=H, t=2)
            x1 = qv[:, :, 0, :]
            x2 = qv[:, :, 1, :]
            cosv = sb_cos[:, tt * 64:(tt + 1) * 64].unsqueeze(1).broadcast_to([128, H, 64])
            sinv = sb_sin[:, tt * 64:(tt + 1) * 64].unsqueeze(1).broadcast_to([128, H, 64])
            ov = sb_qb[:, :].rearrange("p (h t j) -> p h t j", h=H, t=2)
            ta = sb_ta[:, :].rearrange("p (h j) -> p h j", h=H)
            tb = sb_tb[:, :].rearrange("p (h j) -> p h j", h=H)
            p.op("V", lambda e, ta=ta, x1=x1, cosv=cosv: e.tensor_tensor(ta, x1, cosv, ALU.mult),
                 (key_qf, "cos"), ("rta",))
            p.op("V", lambda e, tb=tb, x2=x2, sinv=sinv: e.tensor_tensor(tb, x2, sinv, ALU.mult),
                 (key_qf, "sin"), ("rtb",))
            p.op("V", lambda e, ov=ov, ta=ta, tb=tb: e.tensor_tensor(ov[:, :, 0, :], ta, tb, ALU.subtract),
                 ("rta", "rtb"), (key_qb,))
            p.op("V", lambda e, ta=ta, x1=x1, sinv=sinv: e.tensor_tensor(ta, x1, sinv, ALU.mult),
                 (key_qf, "sin"), ("rta",))
            p.op("V", lambda e, tb=tb, x2=x2, cosv=cosv: e.tensor_tensor(tb, x2, cosv, ALU.mult),
                 (key_qf, "cos"), ("rtb",))
            p.op("V", lambda e, ov=ov, ta=ta, tb=tb: e.tensor_tensor(ov[:, :, 1, :], ta, tb, ALU.add),
                 ("rta", "rtb"), (key_qb,))
            p.dma("G", lambda e, sb_qb=sb_qb, tt=tt, dst=dst: e.dma_start(
                dst[tt * 128:(tt + 1) * 128, :], sb_qb[:, :]),
                (key_qb,), (dst_key,), st_sem)

        def layer_norm(zkey, zview, outkey, outview, pfx, sb_t, sb_st, wextra=()):
            p.op("V", lambda e, zview=zview, sb_st=sb_st: e.reduce_sum(
                sb_st[:, 0:1], zview, axis=AX.X), (zkey,), (pfx + "st0",))
            p.op("S", lambda e, sb_st=sb_st: e.mul(sb_st[:, 1:2], sb_st[:, 0:1], f32(1.0 / D)),
                 (pfx + "st0",), (pfx + "st1",))
            p.op("V", lambda e, sb_t=sb_t, zview=zview, sb_st=sb_st: e.tensor_scalar(
                sb_t[:, :], zview, sb_st[:, 1:2], None, ALU.subtract),
                (zkey, pfx + "st1"), (pfx + "t",))
            p.op("V", lambda e, sb_t=sb_t, zview=zview, sb_st=sb_st: e.scalar_tensor_tensor(
                zview, sb_t[:, :], 1.0, sb_t[:, :], ALU.mult, ALU.mult,
                accum_out=sb_st[:, 2:3]), (pfx + "t",), (zkey, pfx + "st2"))
            p.op("S", lambda e, sb_st=sb_st: e.activation(
                sb_st[:, 3:4], sb_st[:, 2:3], AF.Ln, bias=f32(1e-5), scale=f32(1.0 / D)),
                (pfx + "st2",), (pfx + "st3",))
            p.op("S", lambda e, sb_st=sb_st: e.activation(
                sb_st[:, 4:5], sb_st[:, 3:4], AF.Exp, scale=f32(-0.5)),
                (pfx + "st3",), (pfx + "st4",))
            p.op("V", lambda e, outview=outview, sb_t=sb_t, sb_st=sb_st: e.tensor_scalar(
                outview, sb_t[:, :], sb_st[:, 4:5], None, ALU.mult),
                (pfx + "t", pfx + "st4"), (outkey,) + tuple(wextra))

        def phase_B1():
            ar = Arena(MARK_SIN)
            sb_wdn = ar.al("sb_wdn", [128, NKT * NLT * 128], BF16)
            sb_xT = [ar.al(f"sb_xTa{i}", [128, NKT * CH], BF16) for i in range(2)]
            sb_ck = [ar.al(f"sb_ck{i}", [128, NLT * CH], BF16) for i in range(2)]
            ps_ck = [PS[0], PS[1]]

            p.dma("G", lambda e: e.dma_start(sb_wdn[:, :], d_wdn.rearrange("p k m c -> p (k m c)")),
                  (), ("wdn",), "d_wdn")
            for c in range(NCH):
                sl = c % 2
                for kk2 in range(2):
                    kt = c * 2 + kk2
                    p.dma("G", lambda e, kt=kt: e.dma_start(
                        sb_wq_g[:, kt * D:(kt + 1) * D], d_wq[:, kt, :]),
                        (), ("wq",), "d_wq")
                load_xT(sb_xT[sl], f"xTa{sl}", s_xbf, c, xbf_keys(c))
                for mt in range(NLT):
                    pp = mt % 2
                    def mmg(e, sl=sl, mt=mt, pp=pp, sb_wdn=sb_wdn, sb_xT=sb_xT, ps_ck=ps_ck):
                        for kt in range(NKT):
                            ins = e.matmul(ps_ck[pp][:, :],
                                           sb_wdn[:, (kt * NLT + mt) * 128:(kt * NLT + mt + 1) * 128],
                                           sb_xT[sl][:, kt * CH:(kt + 1) * CH],
                                           start=(kt == 0), stop=(kt == NKT - 1))
                        return ins
                    p.op("T", mmg, ("wdn", f"xTa{sl}"), (f"psck{pp}",), tag="b1mm")
                    p.op("V", lambda e, sl=sl, mt=mt, pp=pp, sb_ck=sb_ck, ps_ck=ps_ck: e.tensor_copy(
                        sb_ck[sl][:, mt * CH:(mt + 1) * CH], ps_ck[pp][:, :]),
                        (f"psck{pp}",), (f"ck{sl}",), tag="b1ev")
                for mt in range(NLT):
                    p.dma("G", lambda e, c=c, mt=mt, sl=sl, sb_ck=sb_ck: e.dma_start(
                        s_ckvT[mt * 128:(mt + 1) * 128, c * CH:(c + 1) * CH],
                        sb_ck[sl][:, mt * CH:(mt + 1) * CH]),
                        (f"ck{sl}",), ("ckvT",), f"d_ckst{sl}")
            if "dbg" in _TAPS:
                dbg_w = dram("dbg_wdn", (128, NKT * NLT * 128), BF16, "ExternalOutput")
                dbg_x = dram("dbg_xT", (128, NKT * CH), BF16, "ExternalOutput")
                p.dma("G", lambda e, dbg_w=dbg_w, sb_wdn=sb_wdn: e.dma_start(dbg_w, sb_wdn[:, :]),
                      ("wdn",), ("dbgw",), "d_dbgw")
                p.dma("G", lambda e, dbg_x=dbg_x, sb_xT=sb_xT: e.dma_start(dbg_x, sb_xT[1][:, :]),
                      ("xTa1",), ("dbgx",), "d_dbgx")
            p.barrier()

        def phase_B2():
            ar = Arena(MARK_SIN)
            sb_wq = sb_wq_g
            if "B1" not in _PHASES:
                p.dma("G", lambda e: e.dma_start(sb_wq_g[:, :], d_wq.rearrange("p k c -> p (k c)")),
                      (), ("wq",), "d_wq")
            sb_xT = [ar.al(f"sb_xTq{i}", [128, NKT * CH], BF16) for i in range(2)]
            sb_qf = [ar.al(f"sb_qf{i}", [128, D], F32) for i in range(2)]
            sb_qb = [ar.al(f"sb_qb{i}", [128, D], BF16) for i in range(2)]
            sb_ta = ar.al("sb_ta", [128, H * 64], F32)
            sb_tb = ar.al("sb_tb", [128, H * 64], F32)
            ps_q = [PS[0], PS[1]]

            p.dma("G", lambda e: e.dma_start(sb_wq[:, :], d_wq.rearrange("p k c -> p (k c)")),
                  (), ("wq",), "d_wq")
            for c in range(NCH):
                sl = c % 2
                load_xT(sb_xT[sl], f"xTq{sl}", s_xbf, c, xbf_keys(c))
                for mt in range(4):
                    msl = mt % 2
                    tt = c * 4 + mt
                    for nchk in range(4):
                        pp = nchk % 2
                        def mmg(e, sl=sl, mt=mt, nchk=nchk, pp=pp, sb_xT=sb_xT, sb_wq=sb_wq, ps_q=ps_q):
                            for kt in range(NKT):
                                ins = e.matmul(ps_q[pp][:, :],
                                               sb_xT[sl][:, kt * CH + mt * 128:kt * CH + (mt + 1) * 128],
                                               sb_wq[:, kt * D + nchk * CH:kt * D + (nchk + 1) * CH],
                                               start=(kt == 0), stop=(kt == NKT - 1))
                            return ins
                        p.op("T", mmg, (f"xTq{sl}", "wq"), (f"psq{pp}",), tag="b2mm")
                        p.op("V", lambda e, msl=msl, nchk=nchk, pp=pp, sb_qf=sb_qf, ps_q=ps_q: e.tensor_copy(
                            sb_qf[msl][:, nchk * CH:(nchk + 1) * CH], ps_q[pp][:, :]),
                            (f"psq{pp}",), (f"qf{msl}",), tag="b2ev")
                    rope_and_store(f"qf{msl}", sb_qf[msl], sb_ta, sb_tb,
                                   sb_qb[msl], f"qb{msl}", tt, s_qrow, "qrow", f"d_qst{msl}")
            p.barrier()

        def phase_B3():
            ar = Arena(MARK_SIN)
            sb_wuk = ar.al("sb_wuk", [128, NLT * D], BF16)
            sb_wuv = ar.al("sb_wuv", [128, NLT * D], BF16)
            sb_cv = [ar.al(f"sb_cv{i}", [128, NLT * CH], BF16) for i in range(2)]
            sb_kf = [ar.al(f"sb_kf{i}", [128, D], F32) for i in range(2)]
            sb_kb = [ar.al(f"sb_kb{i}", [128, D], BF16) for i in range(2)]
            sb_vb = [ar.al(f"sb_vb{i}", [128, D], BF16) for i in range(2)]
            sb_ta = ar.al("sb_ta3", [128, H * 64], F32)
            sb_tb = ar.al("sb_tb3", [128, H * 64], F32)
            ps_k = [PS[0], PS[1]]
            ps_v = [PS[2], PS[3]]

            p.dma("G", lambda e: e.dma_start(sb_wuk[:, :], d_wuk.rearrange("p k c -> p (k c)")),
                  (), ("wuk",), "d_wuk")
            p.dma("G", lambda e: e.dma_start(sb_wuv[:, :], d_wuv.rearrange("p k c -> p (k c)")),
                  (), ("wuv",), "d_wuv")
            for c in range(NCH):
                sl = c % 2
                for kk2 in range(2):
                    kt = c * 2 + kk2
                    p.dma("G", lambda e, kt=kt: e.dma_start(
                        sb_wo_g[:, kt * D:(kt + 1) * D], d_wo[:, kt, :]),
                        (), ("wo",), "d_wo")
                fns = [
                    (lambda lt: (lambda e, sb_cv=sb_cv, sl=sl, c=c: e.dma_start(
                        sb_cv[sl][:, lt * CH:(lt + 1) * CH],
                        s_ckvT[lt * 128:(lt + 1) * 128, c * CH:(c + 1) * CH])))(lt)
                    for lt in range(NLT)]
                p.dma_group("Y", fns, ("ckvT",), f"cv{sl}", f"d_cv{sl}")
                for mt in range(4):
                    msl = mt % 2
                    tt = c * 4 + mt
                    for nchk in range(4):
                        pp = nchk % 2
                        def mmk(e, sl=sl, mt=mt, nchk=nchk, pp=pp, sb_cv=sb_cv, sb_wuk=sb_wuk, ps_k=ps_k):
                            for lt in range(NLT):
                                ins = e.matmul(ps_k[pp][:, :],
                                               sb_cv[sl][:, lt * CH + mt * 128:lt * CH + (mt + 1) * 128],
                                               sb_wuk[:, lt * D + nchk * CH:lt * D + (nchk + 1) * CH],
                                               start=(lt == 0), stop=(lt == NLT - 1))
                            return ins
                        p.op("T", mmk, (f"cv{sl}", "wuk"), (f"psk{pp}",), tag="b3mmk")
                        p.op("V", lambda e, msl=msl, nchk=nchk, pp=pp, sb_kf=sb_kf, ps_k=ps_k: e.tensor_copy(
                            sb_kf[msl][:, nchk * CH:(nchk + 1) * CH], ps_k[pp][:, :]),
                            (f"psk{pp}",), (f"kf{msl}",))
                        def mmv(e, sl=sl, mt=mt, nchk=nchk, pp=pp, sb_cv=sb_cv, sb_wuv=sb_wuv, ps_v=ps_v):
                            for lt in range(NLT):
                                ins = e.matmul(ps_v[pp][:, :],
                                               sb_cv[sl][:, lt * CH + mt * 128:lt * CH + (mt + 1) * 128],
                                               sb_wuv[:, lt * D + nchk * CH:lt * D + (nchk + 1) * CH],
                                               start=(lt == 0), stop=(lt == NLT - 1))
                            return ins
                        p.op("T", mmv, (f"cv{sl}", "wuv"), (f"psv{pp}",), tag="b3mmv")
                        p.op("S", lambda e, msl=msl, nchk=nchk, pp=pp, sb_vb=sb_vb, ps_v=ps_v: e.copy(
                            sb_vb[msl][:, nchk * CH:(nchk + 1) * CH], ps_v[pp][:, :]),
                            (f"psv{pp}",), (f"vb{msl}",))
                    rope_and_store(f"kf{msl}", sb_kf[msl], sb_ta, sb_tb,
                                   sb_kb[msl], f"kb{msl}", tt, s_krow, "krow", f"d_kst{msl}")
                    p.dma("G", lambda e, tt=tt, msl=msl, sb_vb=sb_vb: e.dma_start(
                        s_vrow[tt * 128:(tt + 1) * 128, :], sb_vb[msl][:, :]),
                        (f"vb{msl}",), ("vrow",), f"d_vst{msl}")
            p.barrier()

        def phase_CD():
            ar = Arena(MARK_MASK)
            sb_wo = sb_wo_g
            if "B3" not in _PHASES:
                p.dma("G", lambda e: e.dma_start(sb_wo_g[:, :], d_wo.rearrange("p k c -> p (k c)")),
                      (), ("wo",), "d_wo")
            casts = []
            if "E" in _PHASES:
                casts.append((lambda e: e.dma_start(
                    b_sw1.rearrange("j p c -> (j p) c"),
                    d_sw1.rearrange("j p k c -> (j p) (k c)")), "sw1b"))
                casts.append((lambda e: e.dma_start(
                    b_sw3.rearrange("j p c -> (j p) c"),
                    d_sw3.rearrange("j p k c -> (j p) (k c)")), "sw3b"))
                casts.append((lambda e: e.dma_start(
                    b_sw2.rearrange("j p c -> (j p) c"),
                    d_sw2.rearrange("j p c -> (j p) c")), "sw2b"))
                for ee in range(NE):
                    casts.append(((lambda ee: (lambda e: e.dma_start(
                        b_w1[ee].rearrange("j p c -> (j p) c"),
                        d_w1[ee].rearrange("j p k c -> (j p) (k c)"))))(ee), f"w1b{ee}"))
                    casts.append(((lambda ee: (lambda e: e.dma_start(
                        b_w3[ee].rearrange("j p c -> (j p) c"),
                        d_w3[ee].rearrange("j p k c -> (j p) (k c)"))))(ee), f"w3b{ee}"))
                    casts.append(((lambda ee: (lambda e: e.dma_start(
                        b_w2[ee].rearrange("j p c -> (j p) c"),
                        d_w2[ee].rearrange("j p c -> (j p) c"))))(ee), f"w2b{ee}"))

            if "C" in _PHASES:
                phase_C_inner(ar.off, sb_wo, casts)
                casts = []
            if "D" in _PHASES:
                phase_D_inner(ar.off, sb_wo, casts)

        def phase_C_inner(base, sb_wo, casts):
            arc = Arena(base)
            sb_kT = [arc.al(f"sb_kT{i}", [128, T], BF16) for i in range(2)]
            sb_V = [arc.al(f"sb_V{i}", [128, T], BF16) for i in range(2)]
            sb_qT = [arc.al(f"sb_qT{i}", [128, CH], BF16) for i in range(2)]
            sb_pT = [arc.al(f"sb_pT{i}", [128, CH], BF16) for i in range(8)]
            sb_rc = [arc.al(f"sb_rc{i}", [1, CH], F32) for i in range(2)]
            sb_ao = [arc.al(f"sb_ao{i}", [128, CH], BF16) for i in range(2)]
            ps_S = [PS[0], PS[1], PS[2], PS[3]]
            ps_O = [PS[4], PS[5]]
            ps_L = PS[6]
            ps_B = PS[7]
            DEPTH = 4

            pend_st = []

            def flush_st():
                while pend_st:
                    pend_st.pop(0)()

            for h in range(H):
                hs = h % 2
                p.dma("Y", lambda e, h=h, hs=hs, sb_kT=sb_kT: e.dma_start_transpose(
                    sb_kT[hs][:, :], s_krow[:, h * 128:(h + 1) * 128]),
                    ("krow",), (f"kT{hs}",), f"d_kT{hs}")
                p.dma("Y", lambda e, h=h, hs=hs, sb_V=sb_V: e.dma_start(
                    sb_V[hs][:, :].rearrange("pp (tt c) -> pp tt c", c=128),
                    s_vrow[:, h * 128:(h + 1) * 128].rearrange("(tt pp) c -> pp tt c", pp=128)),
                    ("vrow",), (f"Vh{hs}",), f"d_Vh{hs}")
                if casts and h >= 2:
                    for _ in range(2):
                        if casts:
                            cfn, ckey = casts.pop(0)
                            p.pace("G", "semT")
                            p.dma("G", cfn, (), (ckey,), "d_wcast")
                for qc in range(T // CH):
                    qs = qc % 2
                    opar = (h * (T // CH) + qc) % 2
                    p.dma("Y", lambda e, h=h, qc=qc, qs=qs, sb_qT=sb_qT: e.dma_start_transpose(
                        sb_qT[qs][:, :], s_qrow[qc * CH:(qc + 1) * CH, h * 128:(h + 1) * 128]),
                        ("qrow",), (f"qT{qs}",), f"d_qT{qs}")
                    jmax = qc * 4 + 4

                    def emit_S_mm(j, hs=hs, qs=qs):
                        p.op("T", lambda e, hs=hs, qs=qs, j=j, sb_kT=sb_kT, sb_qT=sb_qT, ps_S=ps_S: e.matmul(
                            ps_S[j % DEPTH][:, :], sb_kT[hs][:, j * 128:(j + 1) * 128],
                            sb_qT[qs][:, :], start=True, stop=True),
                            (f"kT{hs}", f"qT{qs}"), (f"psS{j % DEPTH}",), tag="Smm")

                    def emit_exp(j, qc=qc):
                        p.op("S", lambda e, j=j, sb_pT=sb_pT, ps_S=ps_S: e.activation(
                            sb_pT[j % 8][:, :], ps_S[j % DEPTH][:, :], AF.Exp,
                            bias=f32(EXP_SHIFT), scale=f32(SCALE)),
                            (f"psS{j % DEPTH}",), (f"pT{j % 8}",), tag="exp")
                        if j >= qc * 4:
                            dlt = j - qc * 4
                            p.op("V", lambda e, j=j, dlt=dlt, sb_pT=sb_pT: e.tensor_tensor(
                                sb_pT[j % 8][:, :], sb_pT[j % 8][:, :],
                                sb_mask[:, dlt * CH:(dlt + 1) * CH], ALU.mult),
                                (f"pT{j % 8}", "mask"), (f"pT{j % 8}",), tag="msk")

                    for j in range(min(DEPTH, jmax)):
                        emit_S_mm(j)
                        emit_exp(j)
                    for j in range(jmax):
                        if j == 2:
                            flush_st()
                        if j + DEPTH < jmax:
                            emit_S_mm(j + DEPTH)
                        last = j == jmax - 1
                        p.op("T", lambda e, hs=hs, j=j, last=last, opar=opar, sb_V=sb_V, sb_pT=sb_pT, ps_O=ps_O: e.matmul(
                            ps_O[opar][:, :], sb_V[hs][:, j * 128:(j + 1) * 128], sb_pT[j % 8][:, :],
                            start=(j == 0), stop=last),
                            (f"Vh{hs}", f"pT{j % 8}"), (f"psO{opar}",), tag="AV")
                        p.op("T", lambda e, j=j, last=last, sb_pT=sb_pT, ps_L=ps_L: e.matmul(
                            ps_L[0:1, :], sb_onec[:, :], sb_pT[j % 8][:, :],
                            start=(j == 0), stop=last),
                            ("onec", f"pT{j % 8}"), ("psL",), tag="Lmm")
                        if j + DEPTH < jmax:
                            emit_exp(j + DEPTH)
                    p.op("V", lambda e, qs=qs, sb_rc=sb_rc, ps_L=ps_L: e.reciprocal(
                        sb_rc[qs][:, :], ps_L[0:1, :]), ("psL",), (f"rc{qs}",), tag="rcp")
                    p.op("V", lambda e, qs=qs, opar=opar, sb_ao=sb_ao, ps_O=ps_O: e.tensor_copy(
                        sb_ao[qs][:, :], ps_O[opar][:, :]),
                        (f"psO{opar}",), (f"ao{qs}",), tag="aoev")
                    def st(h=h, qc=qc, qs=qs):
                        p.dma("S", lambda e, h=h, qc=qc, qs=qs, sb_rc=sb_rc: e.dma_start(
                            s_rcs[h:h + 1, qc * CH:(qc + 1) * CH], sb_rc[qs][:, :]),
                            (f"rc{qs}",), ("rcsh",), f"d_rcst{qs}")
                        p.dma("S", lambda e, h=h, qc=qc, qs=qs, sb_ao=sb_ao: e.dma_start(
                            s_aout[h * 128:(h + 1) * 128, qc * CH:(qc + 1) * CH], sb_ao[qs][:, :]),
                            (f"ao{qs}",), ("aout",), f"d_aost{qs}")
                    pend_st.append(st)
            flush_st()
            p.barrier()

        def phase_D_inner(base, sb_wo, casts=()):
            for cfn, ckey in casts:
                p.dma("G", cfn, (), (ckey,), "d_wcast")
            sb_wo = sb_wo_g
            ard = Arena(base)
            sb_aT = [ard.al(f"sb_aT{i}", [128, NKT * CH], BF16) for i in range(2)]
            sb_oh = ard.al("sb_oh", [16, NKT * 128], BF16)
            sb_rc16 = [ard.al(f"sb_rc16{i}", [16, CH], F32) for i in range(2)]
            sb_rc16b = [ard.al(f"sb_rc16b{i}", [16, CH], BF16) for i in range(2)]
            ps_B2 = [PS[2], PS[3]]
            p.dma("G", lambda e: e.dma_start(sb_oh[:, :], d_oh), (), ("oh",), "d_oh")
            sb_xr = [ard.al(f"sb_xr{i}", [128, D], F32) for i in range(2)]
            sb_z = ard.al("sb_z", [128, D], F32)
            sb_t = ard.al("sb_t", [128, D], F32)
            sb_st = ard.al("sb_st", [128, 8], F32)
            sb_xtf = [ard.al(f"sb_xtf{i}", [128, D], F32) for i in range(2)]
            sb_xtb = [ard.al(f"sb_xtb{i}", [128, D], BF16) for i in range(2)]
            ps_a = [PS[0], PS[1]]

            for c in range(T // CH):
                sl = c % 2
                fns = [
                    (lambda kt: (lambda e, c=c, sl=sl, sb_aT=sb_aT: e.dma_start(
                        sb_aT[sl][:, kt * CH:(kt + 1) * CH],
                        s_aout[kt * 128:(kt + 1) * 128, c * CH:(c + 1) * CH])))(kt)
                    for kt in range(NKT)]
                p.dma_group("Y", fns, ("aout",), f"aT{sl}", f"d_aT{sl}")
                p.dma("Y", lambda e, c=c, sl=sl, sb_rc16=sb_rc16: e.dma_start(
                    sb_rc16[sl][:, :], s_rcs[:, c * CH:(c + 1) * CH]),
                    ("rcsh",), (f"rc16_{sl}",), f"d_rc16_{sl}")
                p.op("S", lambda e, sl=sl, sb_rc16=sb_rc16, sb_rc16b=sb_rc16b: e.copy(
                    sb_rc16b[sl][:, :], sb_rc16[sl][:, :]),
                    (f"rc16_{sl}",), (f"rc16b{sl}",))
                for kt in range(NKT):
                    bp = kt % 2
                    p.op("T", lambda e, kt=kt, sl=sl, bp=bp, sb_oh=sb_oh, sb_rc16b=sb_rc16b, ps_B2=ps_B2: e.matmul(
                        ps_B2[bp][:, :], sb_oh[:, kt * 128:(kt + 1) * 128],
                        sb_rc16b[sl][:, :], start=True, stop=True),
                        ("oh", f"rc16b{sl}"), (f"psB2{bp}",), tag="Bmm2")
                    p.op("V", lambda e, kt=kt, sl=sl, bp=bp, sb_aT=sb_aT, ps_B2=ps_B2: e.tensor_tensor(
                        sb_aT[sl][:, kt * CH:(kt + 1) * CH],
                        sb_aT[sl][:, kt * CH:(kt + 1) * CH], ps_B2[bp][:, :], ALU.mult),
                        (f"aT{sl}", f"psB2{bp}"), (f"aT{sl}",), tag="ascl")
                for mt in range(4):
                    msl = mt % 2
                    tt = c * 4 + mt
                    p.dma("Y", lambda e, tt=tt, msl=msl, sb_xr=sb_xr: e.dma_start(
                        sb_xr[msl][:, :], d_x[tt * 128:(tt + 1) * 128, :]),
                        (), (f"xr{msl}",), f"d_xr{msl}")
                    for nchk in range(4):
                        pp = nchk % 2
                        def mmd(e, sl=sl, mt=mt, nchk=nchk, pp=pp, sb_aT=sb_aT, sb_wo=sb_wo, ps_a=ps_a):
                            for kt in range(NKT):
                                ins = e.matmul(ps_a[pp][:, :],
                                               sb_aT[sl][:, kt * CH + mt * 128:kt * CH + (mt + 1) * 128],
                                               sb_wo[:, kt * D + nchk * CH:kt * D + (nchk + 1) * CH],
                                               start=(kt == 0), stop=(kt == NKT - 1))
                            return ins
                        p.op("T", mmd, (f"aT{sl}", "wo"), (f"psa{pp}",), tag="dmm")
                        p.op("V", lambda e, msl=msl, nchk=nchk, pp=pp, sb_z=sb_z, sb_xr=sb_xr, ps_a=ps_a: e.tensor_tensor(
                            sb_z[:, nchk * CH:(nchk + 1) * CH], ps_a[pp][:, :],
                            sb_xr[msl][:, nchk * CH:(nchk + 1) * CH], ALU.add),
                            (f"psa{pp}", f"xr{msl}"), ("z",), tag="res")
                    layer_norm("z", sb_z[:, :], f"xtf{msl}", sb_xtf[msl][:, :], "d1", sb_t, sb_st)
                    p.op("V", lambda e, msl=msl, sb_xtb=sb_xtb, sb_xtf=sb_xtf: e.tensor_copy(
                        sb_xtb[msl][:, :], sb_xtf[msl][:, :]),
                        (f"xtf{msl}",), (f"xtb{msl}",))
                    p.dma("G", lambda e, tt=tt, msl=msl, sb_xtf=sb_xtf: e.dma_start(
                        s_xtf[tt * 128:(tt + 1) * 128, :], sb_xtf[msl][:, :]),
                        (f"xtf{msl}",), ("xtfh",), f"d_xtfst{msl}")
                    p.dma("G", lambda e, tt=tt, msl=msl, sb_xtb=sb_xtb: e.dma_start(
                        s_xtbf[tt * 128:(tt + 1) * 128, :], sb_xtb[msl][:, :]),
                        (f"xtb{msl}",), ("xtbfh",), f"d_xtbst{msl}")
            p.barrier()

        def phase_E0():
            ar = Arena(MARK_SMALL)
            sb_xtT = [ar.al(f"sb_xtTg{i}", [128, NKT * CH], BF16) for i in range(2)]
            sb_sc = ar.al("sb_sc", [128, NE], F32)
            sb_t8 = ar.al("sb_t8", [128, NE], F32)
            sb_i1 = ar.al("sb_i1", [128, NE], F32)
            sb_i2 = ar.al("sb_i2", [128, NE], F32)
            sb_r8 = ar.al("sb_r8", [128, 2], F32)
            sb_bias = ar.al("sb_bias", [128, NE], F32)
            ps_g = [PS[0], PS[1]]
            ps_bb = PS[2]

            p.op("T", lambda e, ps_bb=ps_bb: e.matmul(ps_bb[:, 0:NE], sb_oner[:, :], sb_eb[:, :],
                                                      start=True, stop=True),
                 ("oner", "ebias"), ("psbb",))
            p.op("V", lambda e, sb_bias=sb_bias, ps_bb=ps_bb: e.tensor_copy(
                sb_bias[:, :], ps_bb[:, 0:NE]), ("psbb",), ("bias",))

            for c in range(T // CH):
                sl = c % 2
                load_xT(sb_xtT[sl], f"xtTg{sl}", s_xtbf, c, ("xtbfh",))
                for mt in range(4):
                    tt = c * 4 + mt
                    pp = mt % 2
                    def mmg(e, sl=sl, mt=mt, pp=pp, sb_xtT=sb_xtT, ps_g=ps_g):
                        for kt in range(NKT):
                            ins = e.matmul(ps_g[pp][:, 0:NE],
                                           sb_xtT[sl][:, kt * CH + mt * 128:kt * CH + (mt + 1) * 128],
                                           sb_gate[:, kt * NE:(kt + 1) * NE],
                                           start=(kt == 0), stop=(kt == NKT - 1))
                        return ins
                    p.op("T", mmg, (f"xtTg{sl}", "gate"), (f"psg{pp}",), tag="gmm")
                    p.op("S", lambda e, pp=pp, sb_sc=sb_sc, ps_g=ps_g: e.activation(
                        sb_sc[:, :], ps_g[pp][:, 0:NE], AF.Sigmoid), (f"psg{pp}",), ("sc",))
                    p.op("V", lambda e, sb_t8=sb_t8, sb_sc=sb_sc, sb_bias=sb_bias: e.tensor_tensor(
                        sb_t8[:, :], sb_sc[:, :], sb_bias[:, :], ALU.add), ("sc", "bias"), ("t8",))
                    p.op("V", lambda e, sb_r8=sb_r8, sb_t8=sb_t8: e.reduce_max(
                        sb_r8[:, 0:1], sb_t8[:, :], axis=AX.X), ("t8",), ("r80",))
                    p.op("V", lambda e, sb_i1=sb_i1, sb_t8=sb_t8, sb_r8=sb_r8: e.tensor_scalar(
                        sb_i1[:, :], sb_t8[:, :], sb_r8[:, 0:1], None, ALU.is_ge),
                        ("t8", "r80"), ("i1",))
                    p.op("V", lambda e, sb_i2=sb_i2, sb_i1=sb_i1, sb_t8=sb_t8: e.scalar_tensor_tensor(
                        sb_i2[:, :], sb_i1[:, :], -1e9, sb_t8[:, :], ALU.mult, ALU.add),
                        ("i1", "t8"), ("i2t",))
                    p.op("V", lambda e, sb_r8=sb_r8, sb_i2=sb_i2: e.reduce_max(
                        sb_r8[:, 1:2], sb_i2[:, :], axis=AX.X), ("i2t",), ("r81",))
                    p.op("V", lambda e, sb_i2=sb_i2, sb_r8=sb_r8: e.tensor_scalar(
                        sb_i2[:, :], sb_i2[:, :], sb_r8[:, 1:2], None, ALU.is_ge),
                        ("i2t", "r81"), ("i2",))
                    p.op("V", lambda e, sb_i1=sb_i1, sb_i2=sb_i2: e.tensor_tensor(
                        sb_i1[:, :], sb_i1[:, :], sb_i2[:, :], ALU.add), ("i1", "i2"), ("i1b",))
                    p.op("V", lambda e, tt=tt, sb_sc=sb_sc, sb_i1=sb_i1: e.tensor_tensor(
                        sb_mall[:, tt * NE:(tt + 1) * NE], sb_sc[:, :], sb_i1[:, :], ALU.mult),
                        ("sc", "i1b"), ("mall",))
            p.barrier()

        def phase_E():
            ar = Arena(MARK_SMALL)
            sb_xtT = [ar.al(f"sb_xtT{i}", [128, NKT * CH], BF16) for i in range(2)]
            sb_xm = [ar.al(f"sb_xm{i}", [128, NKT * CH], BF16) for i in range(2)]
            sb_w1p = [ar.al(f"sb_w1p{i}", [128, NKT * 128], BF16) for i in range(2)]
            sb_w3p = [ar.al(f"sb_w3p{i}", [128, NKT * 128], BF16) for i in range(2)]
            sb_w2p = ar.al("sb_w2p", [128, NJ * D], BF16)
            sb_hT = ar.al("sb_hT", [128, NJ * CH], BF16)
            sb_ss = [ar.al(f"sb_ss{i}", [128, CH], BF16) for i in range(2)]
            sb_mr = [ar.al(f"sb_mr{i}", [1, CH], BF16) for i in range(2)]
            sb_nB = [ar.al(f"sb_nBe{i}", [128, CH], BF16) for i in range(2)]
            sb_ya = ar.al("sb_ya", [128, 4 * D], F32)
            sb_t = ar.al("sb_te", [128, D], F32)
            sb_st = ar.al("sb_ste", [128, 8], F32)
            
            ps_u1 = [PS[0], PS[1]]
            ps_u3 = [PS[2], PS[3]]
            ps_y = [PS[4], PS[5]]
            ps_mr = PS[6]
            ps_B = PS[7]

            def prep_mask(mc, ei):
                ee = ei - 1
                par = ei % 2
                sl = mc % 2
                def trg(e, mc=mc, ee=ee, ps_mr=ps_mr):
                    for tt4 in range(4):
                        ins = e.transpose(ps_mr[0:1, tt4 * 128:(tt4 + 1) * 128],
                                          sb_mall[:, (mc * 4 + tt4) * NE + ee:
                                                  (mc * 4 + tt4) * NE + ee + 1],
                                          sb_idf[:, :])
                    return ins
                p.op("T", trg, ("mall", "idf"), ("psmr",), tag="mtr")
                p.op("S", lambda e, par=par, sb_mr=sb_mr, ps_mr=ps_mr: e.copy(
                    sb_mr[par][:, :], ps_mr[0:1, :]), ("psmr",), (f"mr{par}",))
                p.op("T", lambda e, par=par, sb_mr=sb_mr, ps_B=ps_B: e.matmul(
                    ps_B[:, :], sb_oner[:, :], sb_mr[par][:, :], start=True, stop=True),
                    ("oner", f"mr{par}"), ("psBe",), tag="Bemm")
                p.op("S", lambda e, par=par, sb_nB=sb_nB, ps_B=ps_B: e.copy(
                    sb_nB[par][:, :], ps_B[:, :]), ("psBe",), (f"nBe{par}",))
                for kt in range(NKT):
                    p.op("V", lambda e, kt=kt, sl=sl, par=par, sb_xm=sb_xm, sb_xtT=sb_xtT, sb_nB=sb_nB: e.tensor_tensor(
                        sb_xm[par][:, kt * CH:(kt + 1) * CH],
                        sb_xtT[sl][:, kt * CH:(kt + 1) * CH], sb_nB[par][:, :], ALU.mult),
                        (f"xtT{sl}", f"nBe{par}"), (f"xm{par}",), tag="xm")

            pend_out = []
            for mc in range(T // CH):
                sl = mc % 2
                load_xT(sb_xtT[sl], f"xtT{sl}", s_xtbf, mc, ("xtbfh",))
                for ei in range(NE + 1):
                    routed = ei > 0
                    xsl = ei % 2
                    xkey = f"xm{xsl}" if routed else f"xtT{sl}"
                    xsrc = sb_xm[xsl] if routed else sb_xtT[sl]
                    if routed:
                        ee = ei - 1
                        w1src, w3src, w2src = b_w1[ee], b_w3[ee], b_w2[ee]
                        w1key, w3key, w2key = f"w1b{ee}", f"w3b{ee}", f"w2b{ee}"
                    else:
                        w1src, w3src, w2src = b_sw1, b_sw3, b_sw2
                        w1key, w3key, w2key = "sw1b", "sw3b", "sw2b"
                    for j in (0, 1):
                        p.dma("G", lambda e, j=j, w1src=w1src, sb_w1p=sb_w1p: e.dma_start(
                            sb_w1p[j % 2][:, :], w1src[j]),
                            (w1key,), (f"w1p{j % 2}",), f"d_w1p{j % 2}")
                        p.dma("G", lambda e, j=j, w3src=w3src, sb_w3p=sb_w3p: e.dma_start(
                            sb_w3p[j % 2][:, :], w3src[j]),
                            (w3key,), (f"w3p{j % 2}",), f"d_w3p{j % 2}")
                    fns = [
                        (lambda j: (lambda e, w2src=w2src, sb_w2p=sb_w2p: e.dma_start(
                            sb_w2p[:, j * D:(j + 1) * D], w2src[j])))(j)
                        for j in range(NJ)]
                    p.dma_group("G", fns, (w2key,), "w2p", "d_w2p")
                    for j in range(NJ):
                        ws = j % 2
                        def mmu1(e, ws=ws, xsrc=xsrc, sb_w1p=sb_w1p, ps_u1=ps_u1):
                            for kt in range(NKT):
                                ins = e.matmul(ps_u1[ws][:, :],
                                               sb_w1p[ws][:, kt * 128:(kt + 1) * 128],
                                               xsrc[:, kt * CH:(kt + 1) * CH],
                                               start=(kt == 0), stop=(kt == NKT - 1))
                            return ins
                        p.op("T", mmu1, (f"w1p{ws}", xkey), (f"psu1{ws}",), tag="u1")
                        def mmu3(e, ws=ws, xsrc=xsrc, sb_w3p=sb_w3p, ps_u3=ps_u3):
                            for kt in range(NKT):
                                ins = e.matmul(ps_u3[ws][:, :],
                                               sb_w3p[ws][:, kt * 128:(kt + 1) * 128],
                                               xsrc[:, kt * CH:(kt + 1) * CH],
                                               start=(kt == 0), stop=(kt == NKT - 1))
                            return ins
                        p.op("T", mmu3, (f"w3p{ws}", xkey), (f"psu3{ws}",), tag="u3")
                        if j + 2 < NJ:
                            p.dma("G", lambda e, j=j, w1src=w1src, sb_w1p=sb_w1p: e.dma_start(
                                sb_w1p[j % 2][:, :], w1src[j + 2]),
                                (w1key,), (f"w1p{j % 2}",), f"d_w1p{j % 2}")
                            p.dma("G", lambda e, j=j, w3src=w3src, sb_w3p=sb_w3p: e.dma_start(
                                sb_w3p[j % 2][:, :], w3src[j + 2]),
                                (w3key,), (f"w3p{j % 2}",), f"d_w3p{j % 2}")
                        p.op("S", lambda e, ws=ws, sb_ss=sb_ss, ps_u1=ps_u1: e.activation(
                            sb_ss[ws][:, :], ps_u1[ws][:, :], AF.Silu),
                            (f"psu1{ws}",), (f"ss{ws}",), tag="silu")
                        p.op("V", lambda e, j=j, ws=ws, sb_hT=sb_hT, sb_ss=sb_ss, ps_u3=ps_u3: e.tensor_tensor(
                            sb_hT[:, j * CH:(j + 1) * CH], sb_ss[ws][:, :], ps_u3[ws][:, :], ALU.mult),
                            (f"ss{ws}", f"psu3{ws}"), (f"hT{j}",), tag="h")
                    if ei == 0:
                        while pend_out:
                            pend_out.pop(0)()
                        fns = [
                            (lambda tt4: (lambda e, mc=mc, sb_ya=sb_ya: e.dma_start(
                                sb_ya[:, tt4 * D:(tt4 + 1) * D],
                                s_xtf[(mc * 4 + tt4) * 128:(mc * 4 + tt4 + 1) * 128, :])))(tt4)
                            for tt4 in range(4)]
                        p.dma_group("Y", fns, ("xtfh",), "ya", "d_ya")
                    if ei < NE:
                        prep_mask(mc, ei + 1)
                    for tt4 in range(4):
                        for nchk in range(4):
                            pp = (tt4 * 4 + nchk) % 2
                            def mmy(e, tt4=tt4, nchk=nchk, pp=pp, sb_hT=sb_hT, sb_w2p=sb_w2p, ps_y=ps_y):
                                for j in range(NJ):
                                    ins = e.matmul(ps_y[pp][:, :],
                                                   sb_hT[:, j * CH + tt4 * 128:j * CH + (tt4 + 1) * 128],
                                                   sb_w2p[:, j * D + nchk * CH:j * D + (nchk + 1) * CH],
                                                   start=(j == 0), stop=(j == NJ - 1))
                                return ins
                            p.op("T", mmy, tuple(f"hT{j}" for j in range(NJ)) + ("w2p",),
                                 (f"psy{pp}",), tag="ymm")
                            p.op("V", lambda e, tt4=tt4, nchk=nchk, pp=pp, sb_ya=sb_ya, ps_y=ps_y: e.tensor_tensor(
                                sb_ya[:, tt4 * D + nchk * CH:tt4 * D + (nchk + 1) * CH],
                                sb_ya[:, tt4 * D + nchk * CH:tt4 * D + (nchk + 1) * CH],
                                ps_y[pp][:, :], ALU.add),
                                (f"psy{pp}", "ya"), ("ya",), tag="yacc")
                for tt4 in range(4):
                    tt = mc * 4 + tt4
                    osl = tt4
                    layer_norm("ya", sb_ya[:, tt4 * D:(tt4 + 1) * D], f"o{osl}",
                               sb_ya[:, tt4 * D:(tt4 + 1) * D], "e2", sb_t, sb_st,
                               wextra=("ya",))

                    def ost(tt=tt, tt4=tt4, osl=osl):
                        p.dma("G", lambda e, tt=tt, tt4=tt4, sb_ya=sb_ya: e.dma_start(
                            d_out[tt * 128:(tt + 1) * 128, :], sb_ya[:, tt4 * D:(tt4 + 1) * D]),
                            (f"o{osl}", "ya"), ("outh",), f"d_ost{tt4}")
                    pend_out.append(ost)
            while pend_out:
                pend_out.pop(0)()
            p.barrier()

        if "B1" in _PHASES:
            phase_B1()
        if "B2" in _PHASES:
            phase_B2()
        if "B3" in _PHASES:
            phase_B3()
        if "C" in _PHASES or "D" in _PHASES:
            phase_CD()
        if "E0" in _PHASES:
            phase_E0()
        if "E" in _PHASES:
            phase_E()

        with nc.Block() as b0:
            def clr(g):
                nums = sorted(h.num for h in p.handles.values())
                if nums:
                    g.dma_reset(range(min(nums), max(nums) + 1))
                    g.sem_clear(range(min(nums), max(nums) + 1))
            b0.gpsimd(clr)
        nc.all_engine_barrier()
        with nc.Block() as block:
            p.replay(block)

    ninstr = sum(len(v) for v in p.ops.values())
    build_program.last_p = p
    return nc, ninstr


def kernel(L0_attn_x, L0_attn_wq, L0_attn_w_down, L0_attn_w_up, L0_attn_wo,
           L0_ffn_gate_w, L0_ffn_w1, L0_ffn_w2, L0_ffn_w3,
           L0_ffn_sw1, L0_ffn_sw2, L0_ffn_sw3, L0_ffn_expert_bias):
    x = np.ascontiguousarray(np.asarray(L0_attn_x, dtype=np.float32))
    wq = np.asarray(L0_attn_wq, dtype=np.float32)
    w_down = np.asarray(L0_attn_w_down, dtype=np.float32)
    w_up = np.asarray(L0_attn_w_up, dtype=np.float32)
    wo = np.asarray(L0_attn_wo, dtype=np.float32)
    gate_w = np.asarray(L0_ffn_gate_w, dtype=np.float32)
    w1 = np.asarray(L0_ffn_w1, dtype=np.float32)
    w2 = np.asarray(L0_ffn_w2, dtype=np.float32)
    w3 = np.asarray(L0_ffn_w3, dtype=np.float32)
    sw1 = np.asarray(L0_ffn_sw1, dtype=np.float32)
    sw2 = np.asarray(L0_ffn_sw2, dtype=np.float32)
    sw3 = np.asarray(L0_ffn_sw3, dtype=np.float32)
    ebias = np.asarray(L0_ffn_expert_bias, dtype=np.float32)

    perm = _rope_perm()
    w_up_h = w_up.reshape(LAT, H, 2 * DH)
    w_up_k = np.ascontiguousarray(w_up_h[:, :, :DH].reshape(LAT, D))[:, perm]
    w_up_v = np.ascontiguousarray(w_up_h[:, :, DH:].reshape(LAT, D))

    inv_freq = 10000.0 ** (-(np.arange(64, dtype=np.float64)) / 64.0)
    ang = np.arange(T, dtype=np.float64)[:, None] * inv_freq[None, :]
    kk = np.arange(128)[:, None]
    qq = np.arange(CH)[None, :]

    shared = {
        "wq_r": _rhs_tiles(wq[:, perm]), "wdn_l": _lhsT_tiles(w_down),
        "wupk_r": _rhs_tiles(w_up_k), "wupv_r": _rhs_tiles(w_up_v),
        "wo_r": _rhs_tiles(wo), "gate_r": _rhs_tiles(np.ascontiguousarray(gate_w.T)),
        "w1p": np.stack([_panel_lhsT(np.ascontiguousarray(w1[e].T)) for e in range(NE)]),
        "w3p": np.stack([_panel_lhsT(np.ascontiguousarray(w3[e].T)) for e in range(NE)]),
        "w2p": np.stack([_panel_rhs(np.ascontiguousarray(w2[e].T)) for e in range(NE)]),
        "sw1p": _panel_lhsT(np.ascontiguousarray(sw1.T)),
        "sw3p": _panel_lhsT(np.ascontiguousarray(sw3.T)),
        "sw2p": _panel_rhs(np.ascontiguousarray(sw2.T)),
        "ebias": np.ascontiguousarray(ebias.reshape(1, NE)),
        "cosb": np.ascontiguousarray(
            np.cos(ang).astype(np.float32).reshape(T // 128, 128, 64).transpose(1, 0, 2)),
        "sinb": np.ascontiguousarray(
            np.sin(ang).astype(np.float32).reshape(T // 128, 128, 64).transpose(1, 0, 2)),
        "masks": np.ascontiguousarray(
            np.stack([(kk <= qq - 128 * dlt).astype(np.float32) for dlt in range(4)])),
        "onec": np.ones((128, 1), np.float32),
        "onehot": np.ascontiguousarray(
            np.repeat(np.eye(16, dtype=np.float32), 128, axis=1)),
        "oner": np.ones((1, 128), np.float32),
        "identf": np.ascontiguousarray(np.eye(128, dtype=np.float32)),
    }
    in_maps = [dict(shared, x=np.ascontiguousarray(x[b])) for b in range(8)]

    nc, _ = build_program()
    res = run_bass_kernel_spmd(nc, in_maps, core_ids=list(range(8)))
    return np.stack([np.asarray(r["out"], dtype=np.float32) for r in res.results])


if __name__ == "__main__":
    import time
    t0 = time.time()
    nc, n = build_program()
    print(f"built ok: {n} stream entries in {time.time() - t0:.1f}s")


# version 14
# speedup vs baseline: 1.0306x; 1.0306x over previous
import contextlib
import math

import numpy as np

import concourse.bass as bass
import concourse.mybir as mybir
from concourse.bass_utils import run_bass_kernel_spmd

F32 = mybir.dt.float32
BF16 = mybir.dt.bfloat16
AF = mybir.ActivationFunctionType
ALU = mybir.AluOpType
AX = mybir.AxisListType

T = 4096
D = 2048
H = 16
DH = 128
LAT = 1024
HID = 1408
NE = 8
NKT = D // 128
NLT = LAT // 128
NJ = HID // 128
CH = 512
SCALE = 1.0 / math.sqrt(DH)
EXP_SHIFT = -12.0

f32 = np.float32

_TAPS = set()
_PHASES = {"B1", "B2", "B3", "C", "D", "E0", "E"}
_DT_SZ = {F32: 4, BF16: 2}


class Prog:
    ENGS = ("T", "V", "S", "G", "Y")
    ENGSEM = {"T": "semT", "V": "semV", "S": "semS"}
    HAZ = []

    def __init__(self, nc, es):
        self.nc = nc
        self.es = es
        self.ops = {e: [] for e in self.ENGS}
        self.cnt = {}
        self.handles = {}
        self.last_w = {}
        self.readers = {}
        self.wm = {e: {} for e in self.ENGS}
        self.pending = {e: [] for e in self.ENGS}
        self.pending_keys = {}

    def sem(self, name):
        if name not in self.handles:
            self.handles[name] = self.es.enter_context(self.nc.semaphore(name))
            self.cnt[name] = 0
        return self.handles[name]

    def _wait(self, eng, sem, val):
        if self.wm[eng].get(sem, 0) >= val:
            return
        if sem.startswith("d_") and val < self.cnt.get(sem, 0):
            Prog.HAZ.append((eng, sem, val, self.cnt[sem]))
        self.wm[eng][sem] = val
        self.ops[eng].append(("w", sem, val))

    def op(self, eng, fn, reads=(), writes=(), sem=None, inc=None, war=(), tag=""):
        need = {}
        for k in reads:
            if not k:
                continue
            if k in self.pending_keys:
                assert self.pending_keys[k] == eng, f"{eng} reads pending key {k} ({tag})"
            for s, v in self.last_w.get(k, {}).items():
                need[s] = max(need.get(s, 0), v)
        for k in tuple(writes) + tuple(war):
            for s, v in self.last_w.get(k, {}).items():
                need[s] = max(need.get(s, 0), v)
            for s, v in self.readers.get(k, {}).items():
                need[s] = max(need.get(s, 0), v)
        for s, v in sorted(need.items()):
            self._wait(eng, s, v)
        if sem is None:
            sem = self.ENGSEM.get(eng)
        if inc is None:
            inc = 1 if eng in self.ENGSEM else 16
        self.ops[eng].append(("i", fn, sem if inc else None, inc, tag))
        if inc:
            self.sem(sem)
            self.cnt[sem] += inc
            sv, cv = sem, self.cnt[sem]
            items = self.pending[eng] + [(tuple(reads), tuple(writes))]
            self.pending[eng] = []
            for rds, wrs in items:
                for k in wrs:
                    self.pending_keys.pop(k, None)
                    lw = self.last_w.setdefault(k, {})
                    lw[sv] = max(lw.get(sv, 0), cv)
                    self.readers[k] = {}
                for k in rds:
                    if not k:
                        continue
                    rd = self.readers.setdefault(k, {})
                    rd[sv] = max(rd.get(sv, 0), cv)
        else:
            self.pending[eng].append((tuple(reads), tuple(writes)))
            for k in writes:
                self.pending_keys[k] = eng

    def pace(self, eng, sem):
        self._wait(eng, sem, self.cnt.get(sem, 0))

    def dma(self, eng, fn, reads=(), writes=(), slot_sem=None, war=(), tag=""):
        self.op(eng, fn, reads, writes, sem=slot_sem, inc=16, war=war, tag=tag)

    def dma_group(self, eng, fns, reads=(), dst_key=None, slot_sem=None, tag=""):
        n = len(fns)
        for i, fn in enumerate(fns):
            self.dma(eng, fn, reads=reads if i == 0 else (),
                     writes=(dst_key,) if i == n - 1 else (),
                     war=(dst_key,) if i == 0 else (),
                     slot_sem=slot_sem, tag=tag)

    def barrier(self):
        for e in self.ENGS:
            assert not self.pending[e], f"pending ops at barrier on {e}"
        own = {v: k for k, v in self.ENGSEM.items()}
        for e in self.ENGS:
            for s in sorted(self.cnt):
                if own.get(s) == e:
                    continue
                if self.cnt[s] > 0:
                    self._wait(e, s, self.cnt[s])

    def replay(self, block):
        def mk(e):
            def run(eng):
                for entry in self.ops[e]:
                    if entry[0] == "w":
                        eng.wait_ge(self.handles[entry[1]], entry[2])
                    else:
                        _, fn, sem, inc, _tag = entry
                        ins = fn(eng)
                        if sem is not None:
                            ins.then_inc(self.handles[sem], inc)
            return run
        block.tensor(mk("T"))
        block.vector(mk("V"))
        block.scalar(mk("S"))
        block.gpsimd(mk("G"))
        block.sync(mk("Y"))


def _rope_perm():
    perm = np.empty(D, dtype=np.int64)
    for h in range(H):
        for j in range(64):
            perm[h * 128 + j] = h * 128 + 2 * j
            perm[h * 128 + 64 + j] = h * 128 + 2 * j + 1
    return perm


def _lhsT_tiles(w):
    K, M = w.shape
    return np.ascontiguousarray(w.reshape(K // 128, 128, M // 128, 128).transpose(1, 0, 2, 3))


def _rhs_tiles(w):
    K, N = w.shape
    return np.ascontiguousarray(w.reshape(K // 128, 128, N).transpose(1, 0, 2))


def _panel_lhsT(w):
    return np.ascontiguousarray(w.reshape(NKT, 128, NJ, 128).transpose(2, 1, 0, 3))


def _panel_rhs(w):
    return np.ascontiguousarray(w.reshape(NJ, 128, D))


def build_program():
    NCH = T // CH
    NTT = T // 128
    QTR = T // 4
    nc = bass.Bass("TRN2", target_bir_lowering=False)

    def dram(name, shape, dtype, kind="Internal"):
        k = "ExternalOutput" if name in _TAPS and kind == "Internal" else kind
        return nc.dram_tensor(name, list(shape), dtype, kind=k).ap()

    d_x = dram("x", (T, D), F32, "ExternalInput")
    d_wq = dram("wq_r", (128, NKT, D), F32, "ExternalInput")
    d_wdn = dram("wdn_l", (128, NKT, NLT, 128), F32, "ExternalInput")
    d_wuk = dram("wupk_r", (128, NLT, D), F32, "ExternalInput")
    d_wuv = dram("wupv_r", (128, NLT, D), F32, "ExternalInput")
    d_wo = dram("wo_r", (128, NKT, D), F32, "ExternalInput")
    d_gate = dram("gate_r", (128, NKT, NE), F32, "ExternalInput")
    d_w1 = dram("w1p", (NE, NJ, 128, NKT, 128), F32, "ExternalInput")
    d_w3 = dram("w3p", (NE, NJ, 128, NKT, 128), F32, "ExternalInput")
    d_w2 = dram("w2p", (NE, NJ, 128, D), F32, "ExternalInput")
    d_sw1 = dram("sw1p", (NJ, 128, NKT, 128), F32, "ExternalInput")
    d_sw3 = dram("sw3p", (NJ, 128, NKT, 128), F32, "ExternalInput")
    d_sw2 = dram("sw2p", (NJ, 128, D), F32, "ExternalInput")
    d_eb = dram("ebias", (1, NE), F32, "ExternalInput")
    d_cos = dram("cosb", (128, NTT, 64), F32, "ExternalInput")
    d_sin = dram("sinb", (128, NTT, 64), F32, "ExternalInput")
    d_mask = dram("masks", (4, 128, CH), F32, "ExternalInput")
    d_onec = dram("onec", (128, 1), F32, "ExternalInput")
    d_oner = dram("oner", (1, 128), F32, "ExternalInput")
    d_idf = dram("identf", (128, 128), F32, "ExternalInput")
    d_oh = dram("onehot", (16, NKT * 128), F32, "ExternalInput")

    d_out = dram("out", (T, D), F32, "ExternalOutput")

    s_xbf = dram("xbf", (T, D), BF16)
    s_ckvT = dram("ckvTs", (LAT, T), BF16)
    s_qrow = dram("qrow", (T, D), BF16)
    s_krow = dram("krow", (T, D), BF16)
    s_vrow = dram("vrow", (T, D), BF16)
    s_aout = dram("aoutT", (D, T), BF16)
    s_rcs = dram("rcs", (H, T), F32)
    s_xtbf = dram("xtbf", (T, D), BF16)
    s_xtf = dram("xtf", (T, D), F32)
    b_w1 = dram("w1b", (NE, NJ, 128, NKT * 128), BF16)
    b_w3 = dram("w3b", (NE, NJ, 128, NKT * 128), BF16)
    b_w2 = dram("w2b", (NE, NJ, 128, D), BF16)
    b_sw1 = dram("sw1b", (NJ, 128, NKT * 128), BF16)
    b_sw3 = dram("sw3b", (NJ, 128, NKT * 128), BF16)
    b_sw2 = dram("sw2b", (NJ, 128, D), BF16)

    es = contextlib.ExitStack()
    with es:
        p = Prog(nc, es)

        alloc_state = {"off": (int(nc.sbuf_base) + 127) // 128 * 128}

        def salloc(name, shape, dtype, off=None):
            nbytes = int(np.prod(shape[1:])) * _DT_SZ[dtype]
            if off is None:
                off = alloc_state["off"]
                alloc_state["off"] = off + ((nbytes + 127) // 128) * 128
            return nc.alloc_sbuf_tensor_at(name, list(shape), dtype, offset=off)

        sb_onec = salloc("sb_onec", [128, 1], BF16)
        sb_oner = salloc("sb_oner", [1, 128], BF16)
        sb_idf = salloc("sb_idf", [128, 128], F32)
        sb_gate = salloc("sb_gate", [128, NKT * NE], BF16)
        sb_eb = salloc("sb_eb", [1, NE], BF16)
        sb_mall = salloc("sb_mall", [128, NTT * NE], F32)
        MARK_SMALL = alloc_state["off"]
        sb_mask = salloc("sb_mask", [128, 4 * CH], BF16)
        MARK_MASK = alloc_state["off"]
        sb_cos = salloc("sb_cos", [128, NTT * 64], F32)
        sb_sin = salloc("sb_sin", [128, NTT * 64], F32)
        MARK_SIN = alloc_state["off"]
        SBUF_TOP = nc.sbuf_top

        class Arena:
            def __init__(self, start):
                self.off = start
                self.idx = 0

            def al(self, name, shape, dtype):
                nbytes = int(np.prod(shape[1:])) * _DT_SZ[dtype]
                off = self.off
                self.off = off + ((nbytes + 127) // 128) * 128
                assert self.off <= SBUF_TOP, f"SBUF overflow {name}: {self.off} > {SBUF_TOP}"
                self.idx += 1
                return salloc(f"{name}_{self.idx}", list(shape), dtype, off=off)

        PS = [es.enter_context(nc.psum_tensor(f"psbank{i}", [128, CH], F32)) for i in range(8)]
        WQ_OFF = MARK_SIN + 81920
        WO_OFF = 162304
        sb_wq_g = salloc("sb_wq_g", [128, NKT * D], BF16, off=WQ_OFF)
        sb_wo_g = salloc("sb_wo_g", [128, NKT * D], BF16, off=WO_OFF)

        p.dma("Y", lambda e: e.dma_start(sb_cos[:, :], d_cos.rearrange("p t j -> p (t j)")),
              (), ("cos",), "d_cos")
        p.dma("Y", lambda e: e.dma_start(sb_sin[:, :], d_sin.rearrange("p t j -> p (t j)")),
              (), ("sin",), "d_sin")
        p.dma("Y", lambda e: e.dma_start(sb_idf[:, :], d_idf), (), ("idf",), "d_idf")
        p.dma("G", lambda e: e.dma_start(sb_onec[:, :], d_onec), (), ("onec",), "d_onec")
        p.dma("G", lambda e: e.dma_start(sb_oner[:, :], d_oner), (), ("oner",), "d_oner")
        p.dma_group("G", [
            (lambda dlt: (lambda e: e.dma_start(sb_mask[:, dlt * CH:(dlt + 1) * CH], d_mask[dlt])))(dlt)
            for dlt in range(4)], (), "mask", "d_mask")
        p.dma("G", lambda e: e.dma_start(sb_gate[:, :], d_gate.rearrange("p k e -> p (k e)")),
              (), ("gate",), "d_gate")
        p.dma("G", lambda e: e.dma_start(sb_eb[:, :], d_eb), (), ("ebias",), "d_ebias")

        for qq in range(4):
            p.dma("G", lambda e, qq=qq: e.dma_start(
                s_xbf[qq * QTR:(qq + 1) * QTR, :], d_x[qq * QTR:(qq + 1) * QTR, :]),
                (), (f"xbf{qq}",), f"d_xbf{qq}", tag="xcast")

        def xbf_keys(c):
            return tuple(f"xbf{q}" for q in range((c * CH) // QTR, ((c + 1) * CH - 1) // QTR + 1))

        def load_xT(dst, dst_key, src, c, src_keys):
            fns = [
                (lambda kt: (lambda e: e.dma_start_transpose(
                    dst[:, kt * CH:(kt + 1) * CH],
                    src[c * CH:(c + 1) * CH, kt * 128:(kt + 1) * 128])))(kt)
                for kt in range(NKT)]
            p.dma_group("Y", fns, src_keys, dst_key, "d_" + dst_key, tag="xT")

        def rope_and_store(key_qf, sb_qf, sb_ta, sb_tb, sb_qb, key_qb, tt, dst, dst_key, st_sem):
            qv = sb_qf[:, :].rearrange("p (h t j) -> p h t j", h=H, t=2)
            x1 = qv[:, :, 0, :]
            x2 = qv[:, :, 1, :]
            cosv = sb_cos[:, tt * 64:(tt + 1) * 64].unsqueeze(1).broadcast_to([128, H, 64])
            sinv = sb_sin[:, tt * 64:(tt + 1) * 64].unsqueeze(1).broadcast_to([128, H, 64])
            ov = sb_qb[:, :].rearrange("p (h t j) -> p h t j", h=H, t=2)
            ta = sb_ta[:, :].rearrange("p (h j) -> p h j", h=H)
            tb = sb_tb[:, :].rearrange("p (h j) -> p h j", h=H)
            p.op("V", lambda e, ta=ta, x1=x1, cosv=cosv: e.tensor_tensor(ta, x1, cosv, ALU.mult),
                 (key_qf, "cos"), ("rta",))
            p.op("V", lambda e, tb=tb, x2=x2, sinv=sinv: e.tensor_tensor(tb, x2, sinv, ALU.mult),
                 (key_qf, "sin"), ("rtb",))
            p.op("V", lambda e, ov=ov, ta=ta, tb=tb: e.tensor_tensor(ov[:, :, 0, :], ta, tb, ALU.subtract),
                 ("rta", "rtb"), (key_qb,))
            p.op("V", lambda e, ta=ta, x1=x1, sinv=sinv: e.tensor_tensor(ta, x1, sinv, ALU.mult),
                 (key_qf, "sin"), ("rta",))
            p.op("V", lambda e, tb=tb, x2=x2, cosv=cosv: e.tensor_tensor(tb, x2, cosv, ALU.mult),
                 (key_qf, "cos"), ("rtb",))
            p.op("V", lambda e, ov=ov, ta=ta, tb=tb: e.tensor_tensor(ov[:, :, 1, :], ta, tb, ALU.add),
                 ("rta", "rtb"), (key_qb,))
            p.dma("G", lambda e, sb_qb=sb_qb, tt=tt, dst=dst: e.dma_start(
                dst[tt * 128:(tt + 1) * 128, :], sb_qb[:, :]),
                (key_qb,), (dst_key,), st_sem)

        def layer_norm(zkey, zview, outkey, outview, pfx, sb_t, sb_st, wextra=()):
            p.op("V", lambda e, zview=zview, sb_st=sb_st: e.reduce_sum(
                sb_st[:, 0:1], zview, axis=AX.X), (zkey,), (pfx + "st0",))
            p.op("S", lambda e, sb_st=sb_st: e.mul(sb_st[:, 1:2], sb_st[:, 0:1], f32(1.0 / D)),
                 (pfx + "st0",), (pfx + "st1",))
            p.op("V", lambda e, sb_t=sb_t, zview=zview, sb_st=sb_st: e.tensor_scalar(
                sb_t[:, :], zview, sb_st[:, 1:2], None, ALU.subtract),
                (zkey, pfx + "st1"), (pfx + "t",))
            p.op("V", lambda e, sb_t=sb_t, zview=zview, sb_st=sb_st: e.scalar_tensor_tensor(
                zview, sb_t[:, :], 1.0, sb_t[:, :], ALU.mult, ALU.mult,
                accum_out=sb_st[:, 2:3]), (pfx + "t",), (zkey, pfx + "st2"))
            p.op("S", lambda e, sb_st=sb_st: e.activation(
                sb_st[:, 3:4], sb_st[:, 2:3], AF.Ln, bias=f32(1e-5), scale=f32(1.0 / D)),
                (pfx + "st2",), (pfx + "st3",))
            p.op("S", lambda e, sb_st=sb_st: e.activation(
                sb_st[:, 4:5], sb_st[:, 3:4], AF.Exp, scale=f32(-0.5)),
                (pfx + "st3",), (pfx + "st4",))
            p.op("V", lambda e, outview=outview, sb_t=sb_t, sb_st=sb_st: e.tensor_scalar(
                outview, sb_t[:, :], sb_st[:, 4:5], None, ALU.mult),
                (pfx + "t", pfx + "st4"), (outkey,) + tuple(wextra))

        def phase_B1():
            ar = Arena(MARK_SIN)
            sb_wdn = ar.al("sb_wdn", [128, NKT * NLT * 128], BF16)
            sb_xT = [ar.al(f"sb_xTa{i}", [128, NKT * CH], BF16) for i in range(2)]
            sb_ck = [ar.al(f"sb_ck{i}", [128, NLT * CH], BF16) for i in range(2)]
            ps_ck = [PS[0], PS[1]]

            p.dma("G", lambda e: e.dma_start(sb_wdn[:, :], d_wdn.rearrange("p k m c -> p (k m c)")),
                  (), ("wdn",), "d_wdn")
            for c in range(NCH):
                sl = c % 2
                load_xT(sb_xT[sl], f"xTa{sl}", s_xbf, c, xbf_keys(c))
                for mt in range(NLT):
                    pp = mt % 2
                    def mmg(e, sl=sl, mt=mt, pp=pp, sb_wdn=sb_wdn, sb_xT=sb_xT, ps_ck=ps_ck):
                        for kt in range(NKT):
                            ins = e.matmul(ps_ck[pp][:, :],
                                           sb_wdn[:, (kt * NLT + mt) * 128:(kt * NLT + mt + 1) * 128],
                                           sb_xT[sl][:, kt * CH:(kt + 1) * CH],
                                           start=(kt == 0), stop=(kt == NKT - 1))
                        return ins
                    p.op("T", mmg, ("wdn", f"xTa{sl}"), (f"psck{pp}",), tag="b1mm")
                    p.op("V", lambda e, sl=sl, mt=mt, pp=pp, sb_ck=sb_ck, ps_ck=ps_ck: e.tensor_copy(
                        sb_ck[sl][:, mt * CH:(mt + 1) * CH], ps_ck[pp][:, :]),
                        (f"psck{pp}",), (f"ck{sl}",), tag="b1ev")
                for mt in range(NLT):
                    p.dma("G", lambda e, c=c, mt=mt, sl=sl, sb_ck=sb_ck: e.dma_start(
                        s_ckvT[mt * 128:(mt + 1) * 128, c * CH:(c + 1) * CH],
                        sb_ck[sl][:, mt * CH:(mt + 1) * CH]),
                        (f"ck{sl}",), ("ckvT",), f"d_ckst{sl}")
            if "dbg" in _TAPS:
                dbg_w = dram("dbg_wdn", (128, NKT * NLT * 128), BF16, "ExternalOutput")
                dbg_x = dram("dbg_xT", (128, NKT * CH), BF16, "ExternalOutput")
                p.dma("G", lambda e, dbg_w=dbg_w, sb_wdn=sb_wdn: e.dma_start(dbg_w, sb_wdn[:, :]),
                      ("wdn",), ("dbgw",), "d_dbgw")
                p.dma("G", lambda e, dbg_x=dbg_x, sb_xT=sb_xT: e.dma_start(dbg_x, sb_xT[1][:, :]),
                      ("xTa1",), ("dbgx",), "d_dbgx")
            p.barrier()

        def phase_B2():
            ar = Arena(MARK_SIN)
            sb_wq = sb_wq_g
            p.dma("G", lambda e: e.dma_start(sb_wq_g[:, :], d_wq.rearrange("p k c -> p (k c)")),
                  (), ("wq",), "d_wq")
            sb_xT = [ar.al(f"sb_xTq{i}", [128, NKT * CH], BF16) for i in range(2)]
            sb_qf = [ar.al(f"sb_qf{i}", [128, D], F32) for i in range(2)]
            sb_qb = [ar.al(f"sb_qb{i}", [128, D], BF16) for i in range(2)]
            sb_ta = ar.al("sb_ta", [128, H * 64], F32)
            sb_tb = ar.al("sb_tb", [128, H * 64], F32)
            ps_q = [PS[0], PS[1]]

            p.dma("G", lambda e: e.dma_start(sb_wq[:, :], d_wq.rearrange("p k c -> p (k c)")),
                  (), ("wq",), "d_wq")
            for c in range(NCH):
                sl = c % 2
                load_xT(sb_xT[sl], f"xTq{sl}", s_xbf, c, xbf_keys(c))
                for mt in range(4):
                    msl = mt % 2
                    tt = c * 4 + mt
                    for nchk in range(4):
                        pp = nchk % 2
                        def mmg(e, sl=sl, mt=mt, nchk=nchk, pp=pp, sb_xT=sb_xT, sb_wq=sb_wq, ps_q=ps_q):
                            for kt in range(NKT):
                                ins = e.matmul(ps_q[pp][:, :],
                                               sb_xT[sl][:, kt * CH + mt * 128:kt * CH + (mt + 1) * 128],
                                               sb_wq[:, kt * D + nchk * CH:kt * D + (nchk + 1) * CH],
                                               start=(kt == 0), stop=(kt == NKT - 1))
                            return ins
                        p.op("T", mmg, (f"xTq{sl}", "wq"), (f"psq{pp}",), tag="b2mm")
                        p.op("V", lambda e, msl=msl, nchk=nchk, pp=pp, sb_qf=sb_qf, ps_q=ps_q: e.tensor_copy(
                            sb_qf[msl][:, nchk * CH:(nchk + 1) * CH], ps_q[pp][:, :]),
                            (f"psq{pp}",), (f"qf{msl}",), tag="b2ev")
                    rope_and_store(f"qf{msl}", sb_qf[msl], sb_ta, sb_tb,
                                   sb_qb[msl], f"qb{msl}", tt, s_qrow, "qrow", f"d_qst{msl}")
            p.barrier()

        def phase_B3():
            ar = Arena(MARK_SIN)
            sb_wuk = ar.al("sb_wuk", [128, NLT * D], BF16)
            sb_wuv = ar.al("sb_wuv", [128, NLT * D], BF16)
            sb_cv = [ar.al(f"sb_cv{i}", [128, NLT * CH], BF16) for i in range(2)]
            sb_kf = [ar.al(f"sb_kf{i}", [128, D], F32) for i in range(2)]
            sb_kb = [ar.al(f"sb_kb{i}", [128, D], BF16) for i in range(2)]
            sb_vb = [ar.al(f"sb_vb{i}", [128, D], BF16) for i in range(2)]
            sb_ta = ar.al("sb_ta3", [128, H * 64], F32)
            sb_tb = ar.al("sb_tb3", [128, H * 64], F32)
            ps_k = [PS[0], PS[1]]
            ps_v = [PS[2], PS[3]]

            p.dma("G", lambda e: e.dma_start(sb_wuk[:, :], d_wuk.rearrange("p k c -> p (k c)")),
                  (), ("wuk",), "d_wuk")
            p.dma("G", lambda e: e.dma_start(sb_wuv[:, :], d_wuv.rearrange("p k c -> p (k c)")),
                  (), ("wuv",), "d_wuv")
            for c in range(NCH):
                sl = c % 2
                fns = [
                    (lambda lt: (lambda e, sb_cv=sb_cv, sl=sl, c=c: e.dma_start(
                        sb_cv[sl][:, lt * CH:(lt + 1) * CH],
                        s_ckvT[lt * 128:(lt + 1) * 128, c * CH:(c + 1) * CH])))(lt)
                    for lt in range(NLT)]
                p.dma_group("Y", fns, ("ckvT",), f"cv{sl}", f"d_cv{sl}")
                for mt in range(4):
                    msl = mt % 2
                    tt = c * 4 + mt
                    for nchk in range(4):
                        pp = nchk % 2
                        def mmk(e, sl=sl, mt=mt, nchk=nchk, pp=pp, sb_cv=sb_cv, sb_wuk=sb_wuk, ps_k=ps_k):
                            for lt in range(NLT):
                                ins = e.matmul(ps_k[pp][:, :],
                                               sb_cv[sl][:, lt * CH + mt * 128:lt * CH + (mt + 1) * 128],
                                               sb_wuk[:, lt * D + nchk * CH:lt * D + (nchk + 1) * CH],
                                               start=(lt == 0), stop=(lt == NLT - 1))
                            return ins
                        p.op("T", mmk, (f"cv{sl}", "wuk"), (f"psk{pp}",), tag="b3mmk")
                        p.op("V", lambda e, msl=msl, nchk=nchk, pp=pp, sb_kf=sb_kf, ps_k=ps_k: e.tensor_copy(
                            sb_kf[msl][:, nchk * CH:(nchk + 1) * CH], ps_k[pp][:, :]),
                            (f"psk{pp}",), (f"kf{msl}",))
                        def mmv(e, sl=sl, mt=mt, nchk=nchk, pp=pp, sb_cv=sb_cv, sb_wuv=sb_wuv, ps_v=ps_v):
                            for lt in range(NLT):
                                ins = e.matmul(ps_v[pp][:, :],
                                               sb_cv[sl][:, lt * CH + mt * 128:lt * CH + (mt + 1) * 128],
                                               sb_wuv[:, lt * D + nchk * CH:lt * D + (nchk + 1) * CH],
                                               start=(lt == 0), stop=(lt == NLT - 1))
                            return ins
                        p.op("T", mmv, (f"cv{sl}", "wuv"), (f"psv{pp}",), tag="b3mmv")
                        p.op("S", lambda e, msl=msl, nchk=nchk, pp=pp, sb_vb=sb_vb, ps_v=ps_v: e.copy(
                            sb_vb[msl][:, nchk * CH:(nchk + 1) * CH], ps_v[pp][:, :]),
                            (f"psv{pp}",), (f"vb{msl}",))
                    rope_and_store(f"kf{msl}", sb_kf[msl], sb_ta, sb_tb,
                                   sb_kb[msl], f"kb{msl}", tt, s_krow, "krow", f"d_kst{msl}")
                    p.dma("G", lambda e, tt=tt, msl=msl, sb_vb=sb_vb: e.dma_start(
                        s_vrow[tt * 128:(tt + 1) * 128, :], sb_vb[msl][:, :]),
                        (f"vb{msl}",), ("vrow",), f"d_vst{msl}")
            p.barrier()

        def phase_CD():
            ar = Arena(MARK_MASK)
            sb_wo = sb_wo_g
            p.dma("G", lambda e: e.dma_start(sb_wo_g[:, :], d_wo.rearrange("p k c -> p (k c)")),
                  (), ("wo",), "d_wo")
            casts = []
            if "E" in _PHASES:
                casts.append((lambda e: e.dma_start(
                    b_sw1.rearrange("j p c -> (j p) c"),
                    d_sw1.rearrange("j p k c -> (j p) (k c)")), "sw1b"))
                casts.append((lambda e: e.dma_start(
                    b_sw3.rearrange("j p c -> (j p) c"),
                    d_sw3.rearrange("j p k c -> (j p) (k c)")), "sw3b"))
                casts.append((lambda e: e.dma_start(
                    b_sw2.rearrange("j p c -> (j p) c"),
                    d_sw2.rearrange("j p c -> (j p) c")), "sw2b"))
                for ee in range(NE):
                    casts.append(((lambda ee: (lambda e: e.dma_start(
                        b_w1[ee].rearrange("j p c -> (j p) c"),
                        d_w1[ee].rearrange("j p k c -> (j p) (k c)"))))(ee), f"w1b{ee}"))
                    casts.append(((lambda ee: (lambda e: e.dma_start(
                        b_w3[ee].rearrange("j p c -> (j p) c"),
                        d_w3[ee].rearrange("j p k c -> (j p) (k c)"))))(ee), f"w3b{ee}"))
                    casts.append(((lambda ee: (lambda e: e.dma_start(
                        b_w2[ee].rearrange("j p c -> (j p) c"),
                        d_w2[ee].rearrange("j p c -> (j p) c"))))(ee), f"w2b{ee}"))

            if "C" in _PHASES:
                phase_C_inner(ar.off, sb_wo, casts)
                casts = []
            if "D" in _PHASES:
                phase_D_inner(ar.off, sb_wo, casts)

        def phase_C_inner(base, sb_wo, casts):
            arc = Arena(base)
            sb_kT = [arc.al(f"sb_kT{i}", [128, T], BF16) for i in range(2)]
            sb_V = [arc.al(f"sb_V{i}", [128, T], BF16) for i in range(2)]
            sb_qT = [arc.al(f"sb_qT{i}", [128, CH], BF16) for i in range(2)]
            sb_pT = [arc.al(f"sb_pT{i}", [128, CH], BF16) for i in range(8)]
            sb_rc = [arc.al(f"sb_rc{i}", [1, CH], F32) for i in range(2)]
            sb_ao = [arc.al(f"sb_ao{i}", [128, CH], BF16) for i in range(2)]
            ps_S = [PS[0], PS[1], PS[2], PS[3]]
            ps_O = [PS[4], PS[5]]
            ps_L = PS[6]
            ps_B = PS[7]
            DEPTH = 4

            pend_st = []

            def flush_st():
                while pend_st:
                    pend_st.pop(0)()

            for h in range(H):
                hs = h % 2
                p.dma("Y", lambda e, h=h, hs=hs, sb_kT=sb_kT: e.dma_start_transpose(
                    sb_kT[hs][:, :], s_krow[:, h * 128:(h + 1) * 128]),
                    ("krow",), (f"kT{hs}",), f"d_kT{hs}")
                p.dma("Y", lambda e, h=h, hs=hs, sb_V=sb_V: e.dma_start(
                    sb_V[hs][:, :].rearrange("pp (tt c) -> pp tt c", c=128),
                    s_vrow[:, h * 128:(h + 1) * 128].rearrange("(tt pp) c -> pp tt c", pp=128)),
                    ("vrow",), (f"Vh{hs}",), f"d_Vh{hs}")
                if casts and h >= 2:
                    for _ in range(2):
                        if casts:
                            cfn, ckey = casts.pop(0)
                            p.pace("G", "semT")
                            p.dma("G", cfn, (), (ckey,), "d_wcast")
                for qc in range(T // CH):
                    qs = qc % 2
                    opar = (h * (T // CH) + qc) % 2
                    p.dma("Y", lambda e, h=h, qc=qc, qs=qs, sb_qT=sb_qT: e.dma_start_transpose(
                        sb_qT[qs][:, :], s_qrow[qc * CH:(qc + 1) * CH, h * 128:(h + 1) * 128]),
                        ("qrow",), (f"qT{qs}",), f"d_qT{qs}")
                    jmax = qc * 4 + 4

                    def emit_S_mm(j, hs=hs, qs=qs):
                        p.op("T", lambda e, hs=hs, qs=qs, j=j, sb_kT=sb_kT, sb_qT=sb_qT, ps_S=ps_S: e.matmul(
                            ps_S[j % DEPTH][:, :], sb_kT[hs][:, j * 128:(j + 1) * 128],
                            sb_qT[qs][:, :], start=True, stop=True),
                            (f"kT{hs}", f"qT{qs}"), (f"psS{j % DEPTH}",), tag="Smm")

                    def emit_exp(j, qc=qc):
                        p.op("S", lambda e, j=j, sb_pT=sb_pT, ps_S=ps_S: e.activation(
                            sb_pT[j % 8][:, :], ps_S[j % DEPTH][:, :], AF.Exp,
                            bias=f32(EXP_SHIFT), scale=f32(SCALE)),
                            (f"psS{j % DEPTH}",), (f"pT{j % 8}",), tag="exp")
                        if j >= qc * 4:
                            dlt = j - qc * 4
                            p.op("V", lambda e, j=j, dlt=dlt, sb_pT=sb_pT: e.tensor_tensor(
                                sb_pT[j % 8][:, :], sb_pT[j % 8][:, :],
                                sb_mask[:, dlt * CH:(dlt + 1) * CH], ALU.mult),
                                (f"pT{j % 8}", "mask"), (f"pT{j % 8}",), tag="msk")

                    for j in range(min(DEPTH, jmax)):
                        emit_S_mm(j)
                        emit_exp(j)
                    for j in range(jmax):
                        if j == 2:
                            flush_st()
                        if j + DEPTH < jmax:
                            emit_S_mm(j + DEPTH)
                        last = j == jmax - 1
                        p.op("T", lambda e, hs=hs, j=j, last=last, opar=opar, sb_V=sb_V, sb_pT=sb_pT, ps_O=ps_O: e.matmul(
                            ps_O[opar][:, :], sb_V[hs][:, j * 128:(j + 1) * 128], sb_pT[j % 8][:, :],
                            start=(j == 0), stop=last),
                            (f"Vh{hs}", f"pT{j % 8}"), (f"psO{opar}",), tag="AV")
                        p.op("T", lambda e, j=j, last=last, sb_pT=sb_pT, ps_L=ps_L: e.matmul(
                            ps_L[0:1, :], sb_onec[:, :], sb_pT[j % 8][:, :],
                            start=(j == 0), stop=last),
                            ("onec", f"pT{j % 8}"), ("psL",), tag="Lmm")
                        if j + DEPTH < jmax:
                            emit_exp(j + DEPTH)
                    p.op("V", lambda e, qs=qs, sb_rc=sb_rc, ps_L=ps_L: e.reciprocal(
                        sb_rc[qs][:, :], ps_L[0:1, :]), ("psL",), (f"rc{qs}",), tag="rcp")
                    p.op("V", lambda e, qs=qs, opar=opar, sb_ao=sb_ao, ps_O=ps_O: e.tensor_copy(
                        sb_ao[qs][:, :], ps_O[opar][:, :]),
                        (f"psO{opar}",), (f"ao{qs}",), tag="aoev")
                    def st(h=h, qc=qc, qs=qs):
                        p.dma("S", lambda e, h=h, qc=qc, qs=qs, sb_rc=sb_rc: e.dma_start(
                            s_rcs[h:h + 1, qc * CH:(qc + 1) * CH], sb_rc[qs][:, :]),
                            (f"rc{qs}",), ("rcsh",), f"d_rcst{qs}")
                        p.dma("S", lambda e, h=h, qc=qc, qs=qs, sb_ao=sb_ao: e.dma_start(
                            s_aout[h * 128:(h + 1) * 128, qc * CH:(qc + 1) * CH], sb_ao[qs][:, :]),
                            (f"ao{qs}",), ("aout",), f"d_aost{qs}")
                    pend_st.append(st)
            flush_st()
            p.barrier()

        def phase_D_inner(base, sb_wo, casts=()):
            for cfn, ckey in casts:
                p.dma("G", cfn, (), (ckey,), "d_wcast")
            sb_wo = sb_wo_g
            ard = Arena(base)
            sb_aT = [ard.al(f"sb_aT{i}", [128, NKT * CH], BF16) for i in range(2)]
            sb_oh = ard.al("sb_oh", [16, NKT * 128], BF16)
            sb_rc16 = [ard.al(f"sb_rc16{i}", [16, CH], F32) for i in range(2)]
            sb_rc16b = [ard.al(f"sb_rc16b{i}", [16, CH], BF16) for i in range(2)]
            ps_B2 = [PS[2], PS[3]]
            p.dma("G", lambda e: e.dma_start(sb_oh[:, :], d_oh), (), ("oh",), "d_oh")
            sb_xr = [ard.al(f"sb_xr{i}", [128, D], F32) for i in range(2)]
            sb_z = ard.al("sb_z", [128, D], F32)
            sb_t = ard.al("sb_t", [128, D], F32)
            sb_st = ard.al("sb_st", [128, 8], F32)
            sb_xtf = [ard.al(f"sb_xtf{i}", [128, D], F32) for i in range(2)]
            sb_xtb = [ard.al(f"sb_xtb{i}", [128, D], BF16) for i in range(2)]
            ps_a = [PS[0], PS[1]]

            for c in range(T // CH):
                sl = c % 2
                fns = [
                    (lambda kt: (lambda e, c=c, sl=sl, sb_aT=sb_aT: e.dma_start(
                        sb_aT[sl][:, kt * CH:(kt + 1) * CH],
                        s_aout[kt * 128:(kt + 1) * 128, c * CH:(c + 1) * CH])))(kt)
                    for kt in range(NKT)]
                p.dma_group("Y", fns, ("aout",), f"aT{sl}", f"d_aT{sl}")
                p.dma("Y", lambda e, c=c, sl=sl, sb_rc16=sb_rc16: e.dma_start(
                    sb_rc16[sl][:, :], s_rcs[:, c * CH:(c + 1) * CH]),
                    ("rcsh",), (f"rc16_{sl}",), f"d_rc16_{sl}")
                p.op("S", lambda e, sl=sl, sb_rc16=sb_rc16, sb_rc16b=sb_rc16b: e.copy(
                    sb_rc16b[sl][:, :], sb_rc16[sl][:, :]),
                    (f"rc16_{sl}",), (f"rc16b{sl}",))
                for kt in range(NKT):
                    bp = kt % 2
                    p.op("T", lambda e, kt=kt, sl=sl, bp=bp, sb_oh=sb_oh, sb_rc16b=sb_rc16b, ps_B2=ps_B2: e.matmul(
                        ps_B2[bp][:, :], sb_oh[:, kt * 128:(kt + 1) * 128],
                        sb_rc16b[sl][:, :], start=True, stop=True),
                        ("oh", f"rc16b{sl}"), (f"psB2{bp}",), tag="Bmm2")
                    p.op("V", lambda e, kt=kt, sl=sl, bp=bp, sb_aT=sb_aT, ps_B2=ps_B2: e.tensor_tensor(
                        sb_aT[sl][:, kt * CH:(kt + 1) * CH],
                        sb_aT[sl][:, kt * CH:(kt + 1) * CH], ps_B2[bp][:, :], ALU.mult),
                        (f"aT{sl}", f"psB2{bp}"), (f"aT{sl}",), tag="ascl")
                for mt in range(4):
                    msl = mt % 2
                    tt = c * 4 + mt
                    p.dma("Y", lambda e, tt=tt, msl=msl, sb_xr=sb_xr: e.dma_start(
                        sb_xr[msl][:, :], d_x[tt * 128:(tt + 1) * 128, :]),
                        (), (f"xr{msl}",), f"d_xr{msl}")
                    for nchk in range(4):
                        pp = nchk % 2
                        def mmd(e, sl=sl, mt=mt, nchk=nchk, pp=pp, sb_aT=sb_aT, sb_wo=sb_wo, ps_a=ps_a):
                            for kt in range(NKT):
                                ins = e.matmul(ps_a[pp][:, :],
                                               sb_aT[sl][:, kt * CH + mt * 128:kt * CH + (mt + 1) * 128],
                                               sb_wo[:, kt * D + nchk * CH:kt * D + (nchk + 1) * CH],
                                               start=(kt == 0), stop=(kt == NKT - 1))
                            return ins
                        p.op("T", mmd, (f"aT{sl}", "wo"), (f"psa{pp}",), tag="dmm")
                        p.op("V", lambda e, msl=msl, nchk=nchk, pp=pp, sb_z=sb_z, sb_xr=sb_xr, ps_a=ps_a: e.tensor_tensor(
                            sb_z[:, nchk * CH:(nchk + 1) * CH], ps_a[pp][:, :],
                            sb_xr[msl][:, nchk * CH:(nchk + 1) * CH], ALU.add),
                            (f"psa{pp}", f"xr{msl}"), ("z",), tag="res")
                    layer_norm("z", sb_z[:, :], f"xtf{msl}", sb_xtf[msl][:, :], "d1", sb_t, sb_st)
                    p.op("V", lambda e, msl=msl, sb_xtb=sb_xtb, sb_xtf=sb_xtf: e.tensor_copy(
                        sb_xtb[msl][:, :], sb_xtf[msl][:, :]),
                        (f"xtf{msl}",), (f"xtb{msl}",))
                    p.dma("G", lambda e, tt=tt, msl=msl, sb_xtf=sb_xtf: e.dma_start(
                        s_xtf[tt * 128:(tt + 1) * 128, :], sb_xtf[msl][:, :]),
                        (f"xtf{msl}",), ("xtfh",), f"d_xtfst{msl}")
                    p.dma("G", lambda e, tt=tt, msl=msl, sb_xtb=sb_xtb: e.dma_start(
                        s_xtbf[tt * 128:(tt + 1) * 128, :], sb_xtb[msl][:, :]),
                        (f"xtb{msl}",), ("xtbfh",), f"d_xtbst{msl}")
            p.barrier()

        def phase_E0():
            ar = Arena(MARK_SMALL)
            sb_xtT = [ar.al(f"sb_xtTg{i}", [128, NKT * CH], BF16) for i in range(2)]
            sb_sc = ar.al("sb_sc", [128, NE], F32)
            sb_t8 = ar.al("sb_t8", [128, NE], F32)
            sb_i1 = ar.al("sb_i1", [128, NE], F32)
            sb_i2 = ar.al("sb_i2", [128, NE], F32)
            sb_r8 = ar.al("sb_r8", [128, 2], F32)
            sb_bias = ar.al("sb_bias", [128, NE], F32)
            ps_g = [PS[0], PS[1]]
            ps_bb = PS[2]

            p.op("T", lambda e, ps_bb=ps_bb: e.matmul(ps_bb[:, 0:NE], sb_oner[:, :], sb_eb[:, :],
                                                      start=True, stop=True),
                 ("oner", "ebias"), ("psbb",))
            p.op("V", lambda e, sb_bias=sb_bias, ps_bb=ps_bb: e.tensor_copy(
                sb_bias[:, :], ps_bb[:, 0:NE]), ("psbb",), ("bias",))

            for c in range(T // CH):
                sl = c % 2
                load_xT(sb_xtT[sl], f"xtTg{sl}", s_xtbf, c, ("xtbfh",))
                for mt in range(4):
                    tt = c * 4 + mt
                    pp = mt % 2
                    def mmg(e, sl=sl, mt=mt, pp=pp, sb_xtT=sb_xtT, ps_g=ps_g):
                        for kt in range(NKT):
                            ins = e.matmul(ps_g[pp][:, 0:NE],
                                           sb_xtT[sl][:, kt * CH + mt * 128:kt * CH + (mt + 1) * 128],
                                           sb_gate[:, kt * NE:(kt + 1) * NE],
                                           start=(kt == 0), stop=(kt == NKT - 1))
                        return ins
                    p.op("T", mmg, (f"xtTg{sl}", "gate"), (f"psg{pp}",), tag="gmm")
                    p.op("S", lambda e, pp=pp, sb_sc=sb_sc, ps_g=ps_g: e.activation(
                        sb_sc[:, :], ps_g[pp][:, 0:NE], AF.Sigmoid), (f"psg{pp}",), ("sc",))
                    p.op("V", lambda e, sb_t8=sb_t8, sb_sc=sb_sc, sb_bias=sb_bias: e.tensor_tensor(
                        sb_t8[:, :], sb_sc[:, :], sb_bias[:, :], ALU.add), ("sc", "bias"), ("t8",))
                    p.op("V", lambda e, sb_r8=sb_r8, sb_t8=sb_t8: e.reduce_max(
                        sb_r8[:, 0:1], sb_t8[:, :], axis=AX.X), ("t8",), ("r80",))
                    p.op("V", lambda e, sb_i1=sb_i1, sb_t8=sb_t8, sb_r8=sb_r8: e.tensor_scalar(
                        sb_i1[:, :], sb_t8[:, :], sb_r8[:, 0:1], None, ALU.is_ge),
                        ("t8", "r80"), ("i1",))
                    p.op("V", lambda e, sb_i2=sb_i2, sb_i1=sb_i1, sb_t8=sb_t8: e.scalar_tensor_tensor(
                        sb_i2[:, :], sb_i1[:, :], -1e9, sb_t8[:, :], ALU.mult, ALU.add),
                        ("i1", "t8"), ("i2t",))
                    p.op("V", lambda e, sb_r8=sb_r8, sb_i2=sb_i2: e.reduce_max(
                        sb_r8[:, 1:2], sb_i2[:, :], axis=AX.X), ("i2t",), ("r81",))
                    p.op("V", lambda e, sb_i2=sb_i2, sb_r8=sb_r8: e.tensor_scalar(
                        sb_i2[:, :], sb_i2[:, :], sb_r8[:, 1:2], None, ALU.is_ge),
                        ("i2t", "r81"), ("i2",))
                    p.op("V", lambda e, sb_i1=sb_i1, sb_i2=sb_i2: e.tensor_tensor(
                        sb_i1[:, :], sb_i1[:, :], sb_i2[:, :], ALU.add), ("i1", "i2"), ("i1b",))
                    p.op("V", lambda e, tt=tt, sb_sc=sb_sc, sb_i1=sb_i1: e.tensor_tensor(
                        sb_mall[:, tt * NE:(tt + 1) * NE], sb_sc[:, :], sb_i1[:, :], ALU.mult),
                        ("sc", "i1b"), ("mall",))
            p.barrier()

        def phase_E():
            ar = Arena(MARK_SMALL)
            sb_xtT = [ar.al(f"sb_xtT{i}", [128, NKT * CH], BF16) for i in range(2)]
            sb_xm = [ar.al(f"sb_xm{i}", [128, NKT * CH], BF16) for i in range(2)]
            sb_w1p = [ar.al(f"sb_w1p{i}", [128, NKT * 128], BF16) for i in range(2)]
            sb_w3p = [ar.al(f"sb_w3p{i}", [128, NKT * 128], BF16) for i in range(2)]
            sb_w2p = ar.al("sb_w2p", [128, NJ * D], BF16)
            sb_hT = ar.al("sb_hT", [128, NJ * CH], BF16)
            sb_ss = [ar.al(f"sb_ss{i}", [128, CH], BF16) for i in range(2)]
            sb_mr = [ar.al(f"sb_mr{i}", [1, CH], BF16) for i in range(2)]
            sb_nB = [ar.al(f"sb_nBe{i}", [128, CH], BF16) for i in range(2)]
            sb_ya = ar.al("sb_ya", [128, 4 * D], F32)
            sb_t = ar.al("sb_te", [128, D], F32)
            sb_st = ar.al("sb_ste", [128, 8], F32)
            
            ps_u1 = [PS[0], PS[1]]
            ps_u3 = [PS[2], PS[3]]
            ps_y = [PS[4], PS[5]]
            ps_mr = PS[6]
            ps_B = PS[7]

            def prep_mask(mc, ei):
                ee = ei - 1
                par = ei % 2
                sl = mc % 2
                def trg(e, mc=mc, ee=ee, ps_mr=ps_mr):
                    for tt4 in range(4):
                        ins = e.transpose(ps_mr[0:1, tt4 * 128:(tt4 + 1) * 128],
                                          sb_mall[:, (mc * 4 + tt4) * NE + ee:
                                                  (mc * 4 + tt4) * NE + ee + 1],
                                          sb_idf[:, :])
                    return ins
                p.op("T", trg, ("mall", "idf"), ("psmr",), tag="mtr")
                p.op("S", lambda e, par=par, sb_mr=sb_mr, ps_mr=ps_mr: e.copy(
                    sb_mr[par][:, :], ps_mr[0:1, :]), ("psmr",), (f"mr{par}",))
                p.op("T", lambda e, par=par, sb_mr=sb_mr, ps_B=ps_B: e.matmul(
                    ps_B[:, :], sb_oner[:, :], sb_mr[par][:, :], start=True, stop=True),
                    ("oner", f"mr{par}"), ("psBe",), tag="Bemm")
                p.op("S", lambda e, par=par, sb_nB=sb_nB, ps_B=ps_B: e.copy(
                    sb_nB[par][:, :], ps_B[:, :]), ("psBe",), (f"nBe{par}",))
                for kt in range(NKT):
                    p.op("V", lambda e, kt=kt, sl=sl, par=par, sb_xm=sb_xm, sb_xtT=sb_xtT, sb_nB=sb_nB: e.tensor_tensor(
                        sb_xm[par][:, kt * CH:(kt + 1) * CH],
                        sb_xtT[sl][:, kt * CH:(kt + 1) * CH], sb_nB[par][:, :], ALU.mult),
                        (f"xtT{sl}", f"nBe{par}"), (f"xm{par}",), tag="xm")

            pend_out = []
            for mc in range(T // CH):
                sl = mc % 2
                load_xT(sb_xtT[sl], f"xtT{sl}", s_xtbf, mc, ("xtbfh",))
                for ei in range(NE + 1):
                    routed = ei > 0
                    xsl = ei % 2
                    xkey = f"xm{xsl}" if routed else f"xtT{sl}"
                    xsrc = sb_xm[xsl] if routed else sb_xtT[sl]
                    if routed:
                        ee = ei - 1
                        w1src, w3src, w2src = b_w1[ee], b_w3[ee], b_w2[ee]
                        w1key, w3key, w2key = f"w1b{ee}", f"w3b{ee}", f"w2b{ee}"
                    else:
                        w1src, w3src, w2src = b_sw1, b_sw3, b_sw2
                        w1key, w3key, w2key = "sw1b", "sw3b", "sw2b"
                    for j in (0, 1):
                        p.dma("G", lambda e, j=j, w1src=w1src, sb_w1p=sb_w1p: e.dma_start(
                            sb_w1p[j % 2][:, :], w1src[j]),
                            (w1key,), (f"w1p{j % 2}",), f"d_w1p{j % 2}")
                        p.dma("G", lambda e, j=j, w3src=w3src, sb_w3p=sb_w3p: e.dma_start(
                            sb_w3p[j % 2][:, :], w3src[j]),
                            (w3key,), (f"w3p{j % 2}",), f"d_w3p{j % 2}")
                    fns = [
                        (lambda j: (lambda e, w2src=w2src, sb_w2p=sb_w2p: e.dma_start(
                            sb_w2p[:, j * D:(j + 1) * D], w2src[j])))(j)
                        for j in range(NJ)]
                    p.dma_group("G", fns, (w2key,), "w2p", "d_w2p")
                    for j in range(NJ):
                        ws = j % 2
                        def mmu1(e, ws=ws, xsrc=xsrc, sb_w1p=sb_w1p, ps_u1=ps_u1):
                            for kt in range(NKT):
                                ins = e.matmul(ps_u1[ws][:, :],
                                               sb_w1p[ws][:, kt * 128:(kt + 1) * 128],
                                               xsrc[:, kt * CH:(kt + 1) * CH],
                                               start=(kt == 0), stop=(kt == NKT - 1))
                            return ins
                        p.op("T", mmu1, (f"w1p{ws}", xkey), (f"psu1{ws}",), tag="u1")
                        def mmu3(e, ws=ws, xsrc=xsrc, sb_w3p=sb_w3p, ps_u3=ps_u3):
                            for kt in range(NKT):
                                ins = e.matmul(ps_u3[ws][:, :],
                                               sb_w3p[ws][:, kt * 128:(kt + 1) * 128],
                                               xsrc[:, kt * CH:(kt + 1) * CH],
                                               start=(kt == 0), stop=(kt == NKT - 1))
                            return ins
                        p.op("T", mmu3, (f"w3p{ws}", xkey), (f"psu3{ws}",), tag="u3")
                        if j + 2 < NJ:
                            p.dma("G", lambda e, j=j, w1src=w1src, sb_w1p=sb_w1p: e.dma_start(
                                sb_w1p[j % 2][:, :], w1src[j + 2]),
                                (w1key,), (f"w1p{j % 2}",), f"d_w1p{j % 2}")
                            p.dma("G", lambda e, j=j, w3src=w3src, sb_w3p=sb_w3p: e.dma_start(
                                sb_w3p[j % 2][:, :], w3src[j + 2]),
                                (w3key,), (f"w3p{j % 2}",), f"d_w3p{j % 2}")
                        p.op("S", lambda e, ws=ws, sb_ss=sb_ss, ps_u1=ps_u1: e.activation(
                            sb_ss[ws][:, :], ps_u1[ws][:, :], AF.Silu),
                            (f"psu1{ws}",), (f"ss{ws}",), tag="silu")
                        p.op("V", lambda e, j=j, ws=ws, sb_hT=sb_hT, sb_ss=sb_ss, ps_u3=ps_u3: e.tensor_tensor(
                            sb_hT[:, j * CH:(j + 1) * CH], sb_ss[ws][:, :], ps_u3[ws][:, :], ALU.mult),
                            (f"ss{ws}", f"psu3{ws}"), (f"hT{j}",), tag="h")
                    if ei == 0:
                        while pend_out:
                            pend_out.pop(0)()
                        fns = [
                            (lambda tt4: (lambda e, mc=mc, sb_ya=sb_ya: e.dma_start(
                                sb_ya[:, tt4 * D:(tt4 + 1) * D],
                                s_xtf[(mc * 4 + tt4) * 128:(mc * 4 + tt4 + 1) * 128, :])))(tt4)
                            for tt4 in range(4)]
                        p.dma_group("Y", fns, ("xtfh",), "ya", "d_ya")
                    if ei < NE:
                        prep_mask(mc, ei + 1)
                    for tt4 in range(4):
                        for nchk in range(4):
                            pp = (tt4 * 4 + nchk) % 2
                            def mmy(e, tt4=tt4, nchk=nchk, pp=pp, sb_hT=sb_hT, sb_w2p=sb_w2p, ps_y=ps_y):
                                for j in range(NJ):
                                    ins = e.matmul(ps_y[pp][:, :],
                                                   sb_hT[:, j * CH + tt4 * 128:j * CH + (tt4 + 1) * 128],
                                                   sb_w2p[:, j * D + nchk * CH:j * D + (nchk + 1) * CH],
                                                   start=(j == 0), stop=(j == NJ - 1))
                                return ins
                            p.op("T", mmy, tuple(f"hT{j}" for j in range(NJ)) + ("w2p",),
                                 (f"psy{pp}",), tag="ymm")
                            p.op("V", lambda e, tt4=tt4, nchk=nchk, pp=pp, sb_ya=sb_ya, ps_y=ps_y: e.tensor_tensor(
                                sb_ya[:, tt4 * D + nchk * CH:tt4 * D + (nchk + 1) * CH],
                                sb_ya[:, tt4 * D + nchk * CH:tt4 * D + (nchk + 1) * CH],
                                ps_y[pp][:, :], ALU.add),
                                (f"psy{pp}", "ya"), ("ya",), tag="yacc")
                for tt4 in range(4):
                    tt = mc * 4 + tt4
                    osl = tt4
                    layer_norm("ya", sb_ya[:, tt4 * D:(tt4 + 1) * D], f"o{osl}",
                               sb_ya[:, tt4 * D:(tt4 + 1) * D], "e2", sb_t, sb_st,
                               wextra=("ya",))

                    def ost(tt=tt, tt4=tt4, osl=osl):
                        p.dma("G", lambda e, tt=tt, tt4=tt4, sb_ya=sb_ya: e.dma_start(
                            d_out[tt * 128:(tt + 1) * 128, :], sb_ya[:, tt4 * D:(tt4 + 1) * D]),
                            (f"o{osl}", "ya"), ("outh",), f"d_ost{tt4}")
                    pend_out.append(ost)
            while pend_out:
                pend_out.pop(0)()
            p.barrier()

        if "B1" in _PHASES:
            phase_B1()
        if "B2" in _PHASES:
            phase_B2()
        if "B3" in _PHASES:
            phase_B3()
        if "C" in _PHASES or "D" in _PHASES:
            phase_CD()
        if "E0" in _PHASES:
            phase_E0()
        if "E" in _PHASES:
            phase_E()

        with nc.Block() as b0:
            def clr(g):
                nums = sorted(h.num for h in p.handles.values())
                if nums:
                    g.dma_reset(range(min(nums), max(nums) + 1))
                    g.sem_clear(range(min(nums), max(nums) + 1))
            b0.gpsimd(clr)
        nc.all_engine_barrier()
        with nc.Block() as block:
            p.replay(block)

    ninstr = sum(len(v) for v in p.ops.values())
    build_program.last_p = p
    return nc, ninstr


def kernel(L0_attn_x, L0_attn_wq, L0_attn_w_down, L0_attn_w_up, L0_attn_wo,
           L0_ffn_gate_w, L0_ffn_w1, L0_ffn_w2, L0_ffn_w3,
           L0_ffn_sw1, L0_ffn_sw2, L0_ffn_sw3, L0_ffn_expert_bias):
    x = np.ascontiguousarray(np.asarray(L0_attn_x, dtype=np.float32))
    wq = np.asarray(L0_attn_wq, dtype=np.float32)
    w_down = np.asarray(L0_attn_w_down, dtype=np.float32)
    w_up = np.asarray(L0_attn_w_up, dtype=np.float32)
    wo = np.asarray(L0_attn_wo, dtype=np.float32)
    gate_w = np.asarray(L0_ffn_gate_w, dtype=np.float32)
    w1 = np.asarray(L0_ffn_w1, dtype=np.float32)
    w2 = np.asarray(L0_ffn_w2, dtype=np.float32)
    w3 = np.asarray(L0_ffn_w3, dtype=np.float32)
    sw1 = np.asarray(L0_ffn_sw1, dtype=np.float32)
    sw2 = np.asarray(L0_ffn_sw2, dtype=np.float32)
    sw3 = np.asarray(L0_ffn_sw3, dtype=np.float32)
    ebias = np.asarray(L0_ffn_expert_bias, dtype=np.float32)

    perm = _rope_perm()
    w_up_h = w_up.reshape(LAT, H, 2 * DH)
    w_up_k = np.ascontiguousarray(w_up_h[:, :, :DH].reshape(LAT, D))[:, perm]
    w_up_v = np.ascontiguousarray(w_up_h[:, :, DH:].reshape(LAT, D))

    inv_freq = 10000.0 ** (-(np.arange(64, dtype=np.float64)) / 64.0)
    ang = np.arange(T, dtype=np.float64)[:, None] * inv_freq[None, :]
    kk = np.arange(128)[:, None]
    qq = np.arange(CH)[None, :]

    shared = {
        "wq_r": _rhs_tiles(wq[:, perm]), "wdn_l": _lhsT_tiles(w_down),
        "wupk_r": _rhs_tiles(w_up_k), "wupv_r": _rhs_tiles(w_up_v),
        "wo_r": _rhs_tiles(wo), "gate_r": _rhs_tiles(np.ascontiguousarray(gate_w.T)),
        "w1p": np.stack([_panel_lhsT(np.ascontiguousarray(w1[e].T)) for e in range(NE)]),
        "w3p": np.stack([_panel_lhsT(np.ascontiguousarray(w3[e].T)) for e in range(NE)]),
        "w2p": np.stack([_panel_rhs(np.ascontiguousarray(w2[e].T)) for e in range(NE)]),
        "sw1p": _panel_lhsT(np.ascontiguousarray(sw1.T)),
        "sw3p": _panel_lhsT(np.ascontiguousarray(sw3.T)),
        "sw2p": _panel_rhs(np.ascontiguousarray(sw2.T)),
        "ebias": np.ascontiguousarray(ebias.reshape(1, NE)),
        "cosb": np.ascontiguousarray(
            np.cos(ang).astype(np.float32).reshape(T // 128, 128, 64).transpose(1, 0, 2)),
        "sinb": np.ascontiguousarray(
            np.sin(ang).astype(np.float32).reshape(T // 128, 128, 64).transpose(1, 0, 2)),
        "masks": np.ascontiguousarray(
            np.stack([(kk <= qq - 128 * dlt).astype(np.float32) for dlt in range(4)])),
        "onec": np.ones((128, 1), np.float32),
        "onehot": np.ascontiguousarray(
            np.repeat(np.eye(16, dtype=np.float32), 128, axis=1)),
        "oner": np.ones((1, 128), np.float32),
        "identf": np.ascontiguousarray(np.eye(128, dtype=np.float32)),
    }
    in_maps = [dict(shared, x=np.ascontiguousarray(x[b])) for b in range(8)]

    nc, _ = build_program()
    res = run_bass_kernel_spmd(nc, in_maps, core_ids=list(range(8)))
    return np.stack([np.asarray(r["out"], dtype=np.float32) for r in res.results])


if __name__ == "__main__":
    import time
    t0 = time.time()
    nc, n = build_program()
    print(f"built ok: {n} stream entries in {time.time() - t0:.1f}s")


# version 15
# speedup vs baseline: 1.0306x; 1.0001x over previous
import contextlib
import math

import numpy as np

import concourse.bass as bass
import concourse.mybir as mybir
from concourse.bass_utils import run_bass_kernel_spmd

F32 = mybir.dt.float32
BF16 = mybir.dt.bfloat16
AF = mybir.ActivationFunctionType
ALU = mybir.AluOpType
AX = mybir.AxisListType

T = 4096
D = 2048
H = 16
DH = 128
LAT = 1024
HID = 1408
NE = 8
NKT = D // 128
NLT = LAT // 128
NJ = HID // 128
CH = 512
SCALE = 1.0 / math.sqrt(DH)
EXP_SHIFT = -12.0

f32 = np.float32

_TAPS = set()
_PHASES = {"B1", "B2", "B3", "C", "D", "E0", "E"}
_DT_SZ = {F32: 4, BF16: 2}


class Prog:
    ENGS = ("T", "V", "S", "G", "Y")
    ENGSEM = {"T": "semT", "V": "semV", "S": "semS"}
    HAZ = []

    def __init__(self, nc, es):
        self.nc = nc
        self.es = es
        self.ops = {e: [] for e in self.ENGS}
        self.cnt = {}
        self.handles = {}
        self.last_w = {}
        self.readers = {}
        self.wm = {e: {} for e in self.ENGS}
        self.pending = {e: [] for e in self.ENGS}
        self.pending_keys = {}

    def sem(self, name):
        if name not in self.handles:
            self.handles[name] = self.es.enter_context(self.nc.semaphore(name))
            self.cnt[name] = 0
        return self.handles[name]

    def _wait(self, eng, sem, val):
        if self.wm[eng].get(sem, 0) >= val:
            return
        if sem.startswith("d_") and val < self.cnt.get(sem, 0):
            Prog.HAZ.append((eng, sem, val, self.cnt[sem]))
        self.wm[eng][sem] = val
        self.ops[eng].append(("w", sem, val))

    def op(self, eng, fn, reads=(), writes=(), sem=None, inc=None, war=(), tag=""):
        need = {}
        for k in reads:
            if not k:
                continue
            if k in self.pending_keys:
                assert self.pending_keys[k] == eng, f"{eng} reads pending key {k} ({tag})"
            for s, v in self.last_w.get(k, {}).items():
                need[s] = max(need.get(s, 0), v)
        for k in tuple(writes) + tuple(war):
            for s, v in self.last_w.get(k, {}).items():
                need[s] = max(need.get(s, 0), v)
            for s, v in self.readers.get(k, {}).items():
                need[s] = max(need.get(s, 0), v)
        for s, v in sorted(need.items()):
            self._wait(eng, s, v)
        if sem is None:
            sem = self.ENGSEM.get(eng)
        if inc is None:
            inc = 1 if eng in self.ENGSEM else 16
        self.ops[eng].append(("i", fn, sem if inc else None, inc, tag))
        if inc:
            self.sem(sem)
            self.cnt[sem] += inc
            sv, cv = sem, self.cnt[sem]
            items = self.pending[eng] + [(tuple(reads), tuple(writes))]
            self.pending[eng] = []
            for rds, wrs in items:
                for k in wrs:
                    self.pending_keys.pop(k, None)
                    lw = self.last_w.setdefault(k, {})
                    lw[sv] = max(lw.get(sv, 0), cv)
                    self.readers[k] = {}
                for k in rds:
                    if not k:
                        continue
                    rd = self.readers.setdefault(k, {})
                    rd[sv] = max(rd.get(sv, 0), cv)
        else:
            self.pending[eng].append((tuple(reads), tuple(writes)))
            for k in writes:
                self.pending_keys[k] = eng

    def pace(self, eng, sem):
        self._wait(eng, sem, self.cnt.get(sem, 0))

    def dma(self, eng, fn, reads=(), writes=(), slot_sem=None, war=(), tag=""):
        self.op(eng, fn, reads, writes, sem=slot_sem, inc=16, war=war, tag=tag)

    def dma_group(self, eng, fns, reads=(), dst_key=None, slot_sem=None, tag=""):
        n = len(fns)
        for i, fn in enumerate(fns):
            self.dma(eng, fn, reads=reads if i == 0 else (),
                     writes=(dst_key,) if i == n - 1 else (),
                     war=(dst_key,) if i == 0 else (),
                     slot_sem=slot_sem, tag=tag)

    def barrier(self):
        for e in self.ENGS:
            assert not self.pending[e], f"pending ops at barrier on {e}"
        own = {v: k for k, v in self.ENGSEM.items()}
        for e in self.ENGS:
            for s in sorted(self.cnt):
                if own.get(s) == e:
                    continue
                if self.cnt[s] > 0:
                    self._wait(e, s, self.cnt[s])

    def replay(self, block):
        def mk(e):
            def run(eng):
                for entry in self.ops[e]:
                    if entry[0] == "w":
                        eng.wait_ge(self.handles[entry[1]], entry[2])
                    else:
                        _, fn, sem, inc, _tag = entry
                        ins = fn(eng)
                        if sem is not None:
                            ins.then_inc(self.handles[sem], inc)
            return run
        block.tensor(mk("T"))
        block.vector(mk("V"))
        block.scalar(mk("S"))
        block.gpsimd(mk("G"))
        block.sync(mk("Y"))


def _rope_perm():
    perm = np.empty(D, dtype=np.int64)
    for h in range(H):
        for j in range(64):
            perm[h * 128 + j] = h * 128 + 2 * j
            perm[h * 128 + 64 + j] = h * 128 + 2 * j + 1
    return perm


def _lhsT_tiles(w):
    K, M = w.shape
    return np.ascontiguousarray(w.reshape(K // 128, 128, M // 128, 128).transpose(1, 0, 2, 3))


def _rhs_tiles(w):
    K, N = w.shape
    return np.ascontiguousarray(w.reshape(K // 128, 128, N).transpose(1, 0, 2))


def _panel_lhsT(w):
    return np.ascontiguousarray(w.reshape(NKT, 128, NJ, 128).transpose(2, 1, 0, 3))


def _panel_rhs(w):
    return np.ascontiguousarray(w.reshape(NJ, 128, D))


def build_program():
    NCH = T // CH
    NTT = T // 128
    QTR = T // 4
    nc = bass.Bass("TRN2", target_bir_lowering=False)

    def dram(name, shape, dtype, kind="Internal"):
        k = "ExternalOutput" if name in _TAPS and kind == "Internal" else kind
        return nc.dram_tensor(name, list(shape), dtype, kind=k).ap()

    d_x = dram("x", (T, D), F32, "ExternalInput")
    d_wq = dram("wq_r", (128, NKT, D), F32, "ExternalInput")
    d_wdn = dram("wdn_l", (128, NKT, NLT, 128), F32, "ExternalInput")
    d_wuk = dram("wupk_r", (128, NLT, D), F32, "ExternalInput")
    d_wuv = dram("wupv_r", (128, NLT, D), F32, "ExternalInput")
    d_wo = dram("wo_r", (128, NKT, D), F32, "ExternalInput")
    d_gate = dram("gate_r", (128, NKT, NE), F32, "ExternalInput")
    d_w1 = dram("w1p", (NE, NJ, 128, NKT, 128), F32, "ExternalInput")
    d_w3 = dram("w3p", (NE, NJ, 128, NKT, 128), F32, "ExternalInput")
    d_w2 = dram("w2p", (NE, NJ, 128, D), F32, "ExternalInput")
    d_sw1 = dram("sw1p", (NJ, 128, NKT, 128), F32, "ExternalInput")
    d_sw3 = dram("sw3p", (NJ, 128, NKT, 128), F32, "ExternalInput")
    d_sw2 = dram("sw2p", (NJ, 128, D), F32, "ExternalInput")
    d_eb = dram("ebias", (1, NE), F32, "ExternalInput")
    d_cos = dram("cosb", (128, NTT, 64), F32, "ExternalInput")
    d_sin = dram("sinb", (128, NTT, 64), F32, "ExternalInput")
    d_mask = dram("masks", (4, 128, CH), F32, "ExternalInput")
    d_onec = dram("onec", (128, 1), F32, "ExternalInput")
    d_oner = dram("oner", (1, 128), F32, "ExternalInput")
    d_idf = dram("identf", (128, 128), F32, "ExternalInput")
    d_oh = dram("onehot", (16, NKT * 128), F32, "ExternalInput")

    d_out = dram("out", (T, D), F32, "ExternalOutput")

    s_xbf = dram("xbf", (T, D), BF16)
    s_ckvT = dram("ckvTs", (LAT, T), BF16)
    s_qrow = dram("qrow", (T, D), BF16)
    s_krow = dram("krow", (T, D), BF16)
    s_vrow = dram("vrow", (T, D), BF16)
    s_aout = dram("aoutT", (D, T), BF16)
    s_rcs = dram("rcs", (H, T), F32)
    s_xtbf = dram("xtbf", (T, D), BF16)
    s_xtf = dram("xtf", (T, D), F32)
    b_w1 = dram("w1b", (NE, NJ, 128, NKT * 128), BF16)
    b_w3 = dram("w3b", (NE, NJ, 128, NKT * 128), BF16)
    b_w2 = dram("w2b", (NE, NJ, 128, D), BF16)
    b_sw1 = dram("sw1b", (NJ, 128, NKT * 128), BF16)
    b_sw3 = dram("sw3b", (NJ, 128, NKT * 128), BF16)
    b_sw2 = dram("sw2b", (NJ, 128, D), BF16)

    es = contextlib.ExitStack()
    with es:
        p = Prog(nc, es)

        alloc_state = {"off": (int(nc.sbuf_base) + 127) // 128 * 128}

        def salloc(name, shape, dtype, off=None):
            nbytes = int(np.prod(shape[1:])) * _DT_SZ[dtype]
            if off is None:
                off = alloc_state["off"]
                alloc_state["off"] = off + ((nbytes + 127) // 128) * 128
            return nc.alloc_sbuf_tensor_at(name, list(shape), dtype, offset=off)

        sb_onec = salloc("sb_onec", [128, 1], BF16)
        sb_oner = salloc("sb_oner", [1, 128], BF16)
        sb_idf = salloc("sb_idf", [128, 128], F32)
        sb_gate = salloc("sb_gate", [128, NKT * NE], BF16)
        sb_eb = salloc("sb_eb", [1, NE], BF16)
        sb_mall = salloc("sb_mall", [128, NTT * NE], F32)
        MARK_SMALL = alloc_state["off"]
        sb_mask = salloc("sb_mask", [128, 4 * CH], BF16)
        MARK_MASK = alloc_state["off"]
        sb_cos = salloc("sb_cos", [128, NTT * 64], F32)
        sb_sin = salloc("sb_sin", [128, NTT * 64], F32)
        MARK_SIN = alloc_state["off"]
        SBUF_TOP = nc.sbuf_top

        class Arena:
            def __init__(self, start):
                self.off = start
                self.idx = 0

            def al(self, name, shape, dtype):
                nbytes = int(np.prod(shape[1:])) * _DT_SZ[dtype]
                off = self.off
                self.off = off + ((nbytes + 127) // 128) * 128
                assert self.off <= SBUF_TOP, f"SBUF overflow {name}: {self.off} > {SBUF_TOP}"
                self.idx += 1
                return salloc(f"{name}_{self.idx}", list(shape), dtype, off=off)

        PS = [es.enter_context(nc.psum_tensor(f"psbank{i}", [128, CH], F32)) for i in range(8)]

        p.dma("Y", lambda e: e.dma_start(sb_cos[:, :], d_cos.rearrange("p t j -> p (t j)")),
              (), ("cos",), "d_cos")
        p.dma("Y", lambda e: e.dma_start(sb_sin[:, :], d_sin.rearrange("p t j -> p (t j)")),
              (), ("sin",), "d_sin")
        p.dma("Y", lambda e: e.dma_start(sb_idf[:, :], d_idf), (), ("idf",), "d_idf")
        p.dma("G", lambda e: e.dma_start(sb_onec[:, :], d_onec), (), ("onec",), "d_onec")
        p.dma("G", lambda e: e.dma_start(sb_oner[:, :], d_oner), (), ("oner",), "d_oner")
        p.dma_group("G", [
            (lambda dlt: (lambda e: e.dma_start(sb_mask[:, dlt * CH:(dlt + 1) * CH], d_mask[dlt])))(dlt)
            for dlt in range(4)], (), "mask", "d_mask")
        p.dma("G", lambda e: e.dma_start(sb_gate[:, :], d_gate.rearrange("p k e -> p (k e)")),
              (), ("gate",), "d_gate")
        p.dma("G", lambda e: e.dma_start(sb_eb[:, :], d_eb), (), ("ebias",), "d_ebias")

        for qq in range(4):
            p.dma("G", lambda e, qq=qq: e.dma_start(
                s_xbf[qq * QTR:(qq + 1) * QTR, :], d_x[qq * QTR:(qq + 1) * QTR, :]),
                (), (f"xbf{qq}",), f"d_xbf{qq}", tag="xcast")

        def xbf_keys(c):
            return tuple(f"xbf{q}" for q in range((c * CH) // QTR, ((c + 1) * CH - 1) // QTR + 1))

        def load_xT(dst, dst_key, src, c, src_keys):
            fns = [
                (lambda kt: (lambda e: e.dma_start_transpose(
                    dst[:, kt * CH:(kt + 1) * CH],
                    src[c * CH:(c + 1) * CH, kt * 128:(kt + 1) * 128])))(kt)
                for kt in range(NKT)]
            p.dma_group("Y", fns, src_keys, dst_key, "d_" + dst_key, tag="xT")

        def rope_and_store(key_qf, sb_qf, sb_ta, sb_tb, sb_qb, key_qb, tt, dst, dst_key, st_sem):
            qv = sb_qf[:, :].rearrange("p (h t j) -> p h t j", h=H, t=2)
            x1 = qv[:, :, 0, :]
            x2 = qv[:, :, 1, :]
            cosv = sb_cos[:, tt * 64:(tt + 1) * 64].unsqueeze(1).broadcast_to([128, H, 64])
            sinv = sb_sin[:, tt * 64:(tt + 1) * 64].unsqueeze(1).broadcast_to([128, H, 64])
            ov = sb_qb[:, :].rearrange("p (h t j) -> p h t j", h=H, t=2)
            ta = sb_ta[:, :].rearrange("p (h j) -> p h j", h=H)
            tb = sb_tb[:, :].rearrange("p (h j) -> p h j", h=H)
            p.op("V", lambda e, ta=ta, x1=x1, cosv=cosv: e.tensor_tensor(ta, x1, cosv, ALU.mult),
                 (key_qf, "cos"), ("rta",))
            p.op("V", lambda e, tb=tb, x2=x2, sinv=sinv: e.tensor_tensor(tb, x2, sinv, ALU.mult),
                 (key_qf, "sin"), ("rtb",))
            p.op("V", lambda e, ov=ov, ta=ta, tb=tb: e.tensor_tensor(ov[:, :, 0, :], ta, tb, ALU.subtract),
                 ("rta", "rtb"), (key_qb,))
            p.op("V", lambda e, ta=ta, x1=x1, sinv=sinv: e.tensor_tensor(ta, x1, sinv, ALU.mult),
                 (key_qf, "sin"), ("rta",))
            p.op("V", lambda e, tb=tb, x2=x2, cosv=cosv: e.tensor_tensor(tb, x2, cosv, ALU.mult),
                 (key_qf, "cos"), ("rtb",))
            p.op("V", lambda e, ov=ov, ta=ta, tb=tb: e.tensor_tensor(ov[:, :, 1, :], ta, tb, ALU.add),
                 ("rta", "rtb"), (key_qb,))
            p.dma("G", lambda e, sb_qb=sb_qb, tt=tt, dst=dst: e.dma_start(
                dst[tt * 128:(tt + 1) * 128, :], sb_qb[:, :]),
                (key_qb,), (dst_key,), st_sem)

        def layer_norm(zkey, zview, outkey, outview, pfx, sb_t, sb_st):
            p.op("V", lambda e, zview=zview, sb_st=sb_st: e.reduce_sum(
                sb_st[:, 0:1], zview, axis=AX.X), (zkey,), (pfx + "st0",))
            p.op("S", lambda e, sb_st=sb_st: e.mul(sb_st[:, 1:2], sb_st[:, 0:1], f32(1.0 / D)),
                 (pfx + "st0",), (pfx + "st1",))
            p.op("V", lambda e, sb_t=sb_t, zview=zview, sb_st=sb_st: e.tensor_scalar(
                sb_t[:, :], zview, sb_st[:, 1:2], None, ALU.subtract),
                (zkey, pfx + "st1"), (pfx + "t",))
            p.op("V", lambda e, sb_t=sb_t, zview=zview, sb_st=sb_st: e.scalar_tensor_tensor(
                zview, sb_t[:, :], 1.0, sb_t[:, :], ALU.mult, ALU.mult,
                accum_out=sb_st[:, 2:3]), (pfx + "t",), (zkey, pfx + "st2"))
            p.op("S", lambda e, sb_st=sb_st: e.activation(
                sb_st[:, 3:4], sb_st[:, 2:3], AF.Ln, bias=f32(1e-5), scale=f32(1.0 / D)),
                (pfx + "st2",), (pfx + "st3",))
            p.op("S", lambda e, sb_st=sb_st: e.activation(
                sb_st[:, 4:5], sb_st[:, 3:4], AF.Exp, scale=f32(-0.5)),
                (pfx + "st3",), (pfx + "st4",))
            p.op("V", lambda e, outview=outview, sb_t=sb_t, sb_st=sb_st: e.tensor_scalar(
                outview, sb_t[:, :], sb_st[:, 4:5], None, ALU.mult),
                (pfx + "t", pfx + "st4"), (outkey,))

        def phase_B1():
            ar = Arena(MARK_SIN)
            sb_wdn = ar.al("sb_wdn", [128, NKT * NLT * 128], BF16)
            sb_xT = [ar.al(f"sb_xTa{i}", [128, NKT * CH], BF16) for i in range(2)]
            sb_ck = [ar.al(f"sb_ck{i}", [128, NLT * CH], BF16) for i in range(2)]
            ps_ck = [PS[0], PS[1]]

            p.dma("G", lambda e: e.dma_start(sb_wdn[:, :], d_wdn.rearrange("p k m c -> p (k m c)")),
                  (), ("wdn",), "d_wdn")
            for c in range(NCH):
                sl = c % 2
                load_xT(sb_xT[sl], f"xTa{sl}", s_xbf, c, xbf_keys(c))
                for mt in range(NLT):
                    pp = mt % 2
                    def mmg(e, sl=sl, mt=mt, pp=pp, sb_wdn=sb_wdn, sb_xT=sb_xT, ps_ck=ps_ck):
                        for kt in range(NKT):
                            ins = e.matmul(ps_ck[pp][:, :],
                                           sb_wdn[:, (kt * NLT + mt) * 128:(kt * NLT + mt + 1) * 128],
                                           sb_xT[sl][:, kt * CH:(kt + 1) * CH],
                                           start=(kt == 0), stop=(kt == NKT - 1))
                        return ins
                    p.op("T", mmg, ("wdn", f"xTa{sl}"), (f"psck{pp}",), tag="b1mm")
                    p.op("V", lambda e, sl=sl, mt=mt, pp=pp, sb_ck=sb_ck, ps_ck=ps_ck: e.tensor_copy(
                        sb_ck[sl][:, mt * CH:(mt + 1) * CH], ps_ck[pp][:, :]),
                        (f"psck{pp}",), (f"ck{sl}",), tag="b1ev")
                for mt in range(NLT):
                    p.dma("G", lambda e, c=c, mt=mt, sl=sl, sb_ck=sb_ck: e.dma_start(
                        s_ckvT[mt * 128:(mt + 1) * 128, c * CH:(c + 1) * CH],
                        sb_ck[sl][:, mt * CH:(mt + 1) * CH]),
                        (f"ck{sl}",), ("ckvT",), f"d_ckst{sl}")
            if "dbg" in _TAPS:
                dbg_w = dram("dbg_wdn", (128, NKT * NLT * 128), BF16, "ExternalOutput")
                dbg_x = dram("dbg_xT", (128, NKT * CH), BF16, "ExternalOutput")
                p.dma("G", lambda e, dbg_w=dbg_w, sb_wdn=sb_wdn: e.dma_start(dbg_w, sb_wdn[:, :]),
                      ("wdn",), ("dbgw",), "d_dbgw")
                p.dma("G", lambda e, dbg_x=dbg_x, sb_xT=sb_xT: e.dma_start(dbg_x, sb_xT[1][:, :]),
                      ("xTa1",), ("dbgx",), "d_dbgx")
            p.barrier()

        def phase_B2():
            ar = Arena(MARK_SIN)
            sb_wq = ar.al("sb_wq", [128, NKT * D], BF16)
            sb_xT = [ar.al(f"sb_xTq{i}", [128, NKT * CH], BF16) for i in range(2)]
            sb_qf = [ar.al(f"sb_qf{i}", [128, D], F32) for i in range(2)]
            sb_qb = [ar.al(f"sb_qb{i}", [128, D], BF16) for i in range(2)]
            sb_ta = ar.al("sb_ta", [128, H * 64], F32)
            sb_tb = ar.al("sb_tb", [128, H * 64], F32)
            ps_q = [PS[0], PS[1]]

            p.dma("G", lambda e: e.dma_start(sb_wq[:, :], d_wq.rearrange("p k c -> p (k c)")),
                  (), ("wq",), "d_wq")
            for c in range(NCH):
                sl = c % 2
                load_xT(sb_xT[sl], f"xTq{sl}", s_xbf, c, xbf_keys(c))
                for mt in range(4):
                    msl = mt % 2
                    tt = c * 4 + mt
                    for nchk in range(4):
                        pp = nchk % 2
                        def mmg(e, sl=sl, mt=mt, nchk=nchk, pp=pp, sb_xT=sb_xT, sb_wq=sb_wq, ps_q=ps_q):
                            for kt in range(NKT):
                                ins = e.matmul(ps_q[pp][:, :],
                                               sb_xT[sl][:, kt * CH + mt * 128:kt * CH + (mt + 1) * 128],
                                               sb_wq[:, kt * D + nchk * CH:kt * D + (nchk + 1) * CH],
                                               start=(kt == 0), stop=(kt == NKT - 1))
                            return ins
                        p.op("T", mmg, (f"xTq{sl}", "wq"), (f"psq{pp}",), tag="b2mm")
                        p.op("V", lambda e, msl=msl, nchk=nchk, pp=pp, sb_qf=sb_qf, ps_q=ps_q: e.tensor_copy(
                            sb_qf[msl][:, nchk * CH:(nchk + 1) * CH], ps_q[pp][:, :]),
                            (f"psq{pp}",), (f"qf{msl}",), tag="b2ev")
                    rope_and_store(f"qf{msl}", sb_qf[msl], sb_ta, sb_tb,
                                   sb_qb[msl], f"qb{msl}", tt, s_qrow, "qrow", f"d_qst{msl}")
            p.barrier()

        def phase_B3():
            ar = Arena(MARK_SIN)
            sb_wuk = ar.al("sb_wuk", [128, NLT * D], BF16)
            sb_wuv = ar.al("sb_wuv", [128, NLT * D], BF16)
            sb_cv = [ar.al(f"sb_cv{i}", [128, NLT * CH], BF16) for i in range(2)]
            sb_kf = [ar.al(f"sb_kf{i}", [128, D], F32) for i in range(2)]
            sb_kb = [ar.al(f"sb_kb{i}", [128, D], BF16) for i in range(2)]
            sb_vb = [ar.al(f"sb_vb{i}", [128, D], BF16) for i in range(2)]
            sb_ta = ar.al("sb_ta3", [128, H * 64], F32)
            sb_tb = ar.al("sb_tb3", [128, H * 64], F32)
            ps_k = [PS[0], PS[1]]
            ps_v = [PS[2], PS[3]]

            p.dma("G", lambda e: e.dma_start(sb_wuk[:, :], d_wuk.rearrange("p k c -> p (k c)")),
                  (), ("wuk",), "d_wuk")
            p.dma("G", lambda e: e.dma_start(sb_wuv[:, :], d_wuv.rearrange("p k c -> p (k c)")),
                  (), ("wuv",), "d_wuv")
            for c in range(NCH):
                sl = c % 2
                fns = [
                    (lambda lt: (lambda e, sb_cv=sb_cv, sl=sl, c=c: e.dma_start(
                        sb_cv[sl][:, lt * CH:(lt + 1) * CH],
                        s_ckvT[lt * 128:(lt + 1) * 128, c * CH:(c + 1) * CH])))(lt)
                    for lt in range(NLT)]
                p.dma_group("Y", fns, ("ckvT",), f"cv{sl}", f"d_cv{sl}")
                for mt in range(4):
                    msl = mt % 2
                    tt = c * 4 + mt
                    for nchk in range(4):
                        pp = nchk % 2
                        def mmk(e, sl=sl, mt=mt, nchk=nchk, pp=pp, sb_cv=sb_cv, sb_wuk=sb_wuk, ps_k=ps_k):
                            for lt in range(NLT):
                                ins = e.matmul(ps_k[pp][:, :],
                                               sb_cv[sl][:, lt * CH + mt * 128:lt * CH + (mt + 1) * 128],
                                               sb_wuk[:, lt * D + nchk * CH:lt * D + (nchk + 1) * CH],
                                               start=(lt == 0), stop=(lt == NLT - 1))
                            return ins
                        p.op("T", mmk, (f"cv{sl}", "wuk"), (f"psk{pp}",), tag="b3mmk")
                        p.op("V", lambda e, msl=msl, nchk=nchk, pp=pp, sb_kf=sb_kf, ps_k=ps_k: e.tensor_copy(
                            sb_kf[msl][:, nchk * CH:(nchk + 1) * CH], ps_k[pp][:, :]),
                            (f"psk{pp}",), (f"kf{msl}",))
                        def mmv(e, sl=sl, mt=mt, nchk=nchk, pp=pp, sb_cv=sb_cv, sb_wuv=sb_wuv, ps_v=ps_v):
                            for lt in range(NLT):
                                ins = e.matmul(ps_v[pp][:, :],
                                               sb_cv[sl][:, lt * CH + mt * 128:lt * CH + (mt + 1) * 128],
                                               sb_wuv[:, lt * D + nchk * CH:lt * D + (nchk + 1) * CH],
                                               start=(lt == 0), stop=(lt == NLT - 1))
                            return ins
                        p.op("T", mmv, (f"cv{sl}", "wuv"), (f"psv{pp}",), tag="b3mmv")
                        p.op("S", lambda e, msl=msl, nchk=nchk, pp=pp, sb_vb=sb_vb, ps_v=ps_v: e.copy(
                            sb_vb[msl][:, nchk * CH:(nchk + 1) * CH], ps_v[pp][:, :]),
                            (f"psv{pp}",), (f"vb{msl}",))
                    rope_and_store(f"kf{msl}", sb_kf[msl], sb_ta, sb_tb,
                                   sb_kb[msl], f"kb{msl}", tt, s_krow, "krow", f"d_kst{msl}")
                    p.dma("G", lambda e, tt=tt, msl=msl, sb_vb=sb_vb: e.dma_start(
                        s_vrow[tt * 128:(tt + 1) * 128, :], sb_vb[msl][:, :]),
                        (f"vb{msl}",), ("vrow",), f"d_vst{msl}")
            p.barrier()

        def phase_CD():
            ar = Arena(MARK_MASK)
            sb_wo = ar.al("sb_wo", [128, NKT * D], BF16)
            p.dma("G", lambda e: e.dma_start(sb_wo[:, :], d_wo.rearrange("p k c -> p (k c)")),
                  (), ("wo",), "d_wo")
            casts = []
            if "E" in _PHASES:
                casts.append((lambda e: e.dma_start(
                    b_sw1.rearrange("j p c -> (j p) c"),
                    d_sw1.rearrange("j p k c -> (j p) (k c)")), "sw1b"))
                casts.append((lambda e: e.dma_start(
                    b_sw3.rearrange("j p c -> (j p) c"),
                    d_sw3.rearrange("j p k c -> (j p) (k c)")), "sw3b"))
                casts.append((lambda e: e.dma_start(
                    b_sw2.rearrange("j p c -> (j p) c"),
                    d_sw2.rearrange("j p c -> (j p) c")), "sw2b"))
                for ee in range(NE):
                    casts.append(((lambda ee: (lambda e: e.dma_start(
                        b_w1[ee].rearrange("j p c -> (j p) c"),
                        d_w1[ee].rearrange("j p k c -> (j p) (k c)"))))(ee), f"w1b{ee}"))
                    casts.append(((lambda ee: (lambda e: e.dma_start(
                        b_w3[ee].rearrange("j p c -> (j p) c"),
                        d_w3[ee].rearrange("j p k c -> (j p) (k c)"))))(ee), f"w3b{ee}"))
                    casts.append(((lambda ee: (lambda e: e.dma_start(
                        b_w2[ee].rearrange("j p c -> (j p) c"),
                        d_w2[ee].rearrange("j p c -> (j p) c"))))(ee), f"w2b{ee}"))

            if "C" in _PHASES:
                phase_C_inner(ar.off, sb_wo, casts)
                casts = []
            if "D" in _PHASES:
                phase_D_inner(ar.off, sb_wo, casts)

        def phase_C_inner(base, sb_wo, casts):
            arc = Arena(base)
            sb_kT = [arc.al(f"sb_kT{i}", [128, T], BF16) for i in range(2)]
            sb_V = [arc.al(f"sb_V{i}", [128, T], BF16) for i in range(2)]
            sb_qT = [arc.al(f"sb_qT{i}", [128, CH], BF16) for i in range(2)]
            sb_pT = [arc.al(f"sb_pT{i}", [128, CH], BF16) for i in range(8)]
            sb_rc = [arc.al(f"sb_rc{i}", [1, CH], F32) for i in range(2)]
            sb_ao = [arc.al(f"sb_ao{i}", [128, CH], BF16) for i in range(2)]
            ps_S = [PS[0], PS[1], PS[2], PS[3], PS[7]]
            ps_O = [PS[4], PS[5]]
            ps_L = PS[6]
            DEPTH = 5

            pend_st = []

            def flush_st():
                while pend_st:
                    pend_st.pop(0)()

            for h in range(H):
                hs = h % 2
                p.dma("Y", lambda e, h=h, hs=hs, sb_kT=sb_kT: e.dma_start_transpose(
                    sb_kT[hs][:, :], s_krow[:, h * 128:(h + 1) * 128]),
                    ("krow",), (f"kT{hs}",), f"d_kT{hs}")
                p.dma("Y", lambda e, h=h, hs=hs, sb_V=sb_V: e.dma_start(
                    sb_V[hs][:, :].rearrange("pp (tt c) -> pp tt c", c=128),
                    s_vrow[:, h * 128:(h + 1) * 128].rearrange("(tt pp) c -> pp tt c", pp=128)),
                    ("vrow",), (f"Vh{hs}",), f"d_Vh{hs}")
                if casts and h >= 2:
                    for _ in range(2):
                        if casts:
                            cfn, ckey = casts.pop(0)
                            p.pace("G", "semT")
                            p.dma("G", cfn, (), (ckey,), "d_wcast")
                for qc in range(T // CH):
                    qs = qc % 2
                    opar = (h * (T // CH) + qc) % 2
                    p.dma("Y", lambda e, h=h, qc=qc, qs=qs, sb_qT=sb_qT: e.dma_start_transpose(
                        sb_qT[qs][:, :], s_qrow[qc * CH:(qc + 1) * CH, h * 128:(h + 1) * 128]),
                        ("qrow",), (f"qT{qs}",), f"d_qT{qs}")
                    jmax = qc * 4 + 4

                    def emit_S_mm(j, hs=hs, qs=qs):
                        p.op("T", lambda e, hs=hs, qs=qs, j=j, sb_kT=sb_kT, sb_qT=sb_qT, ps_S=ps_S: e.matmul(
                            ps_S[j % DEPTH][:, :], sb_kT[hs][:, j * 128:(j + 1) * 128],
                            sb_qT[qs][:, :], start=True, stop=True),
                            (f"kT{hs}", f"qT{qs}"), (f"psS{j % DEPTH}",), tag="Smm")

                    def emit_exp(j, qc=qc):
                        p.op("S", lambda e, j=j, sb_pT=sb_pT, ps_S=ps_S: e.activation(
                            sb_pT[j % 8][:, :], ps_S[j % DEPTH][:, :], AF.Exp,
                            bias=f32(EXP_SHIFT), scale=f32(SCALE)),
                            (f"psS{j % DEPTH}",), (f"pT{j % 8}",), tag="exp")
                        if j >= qc * 4:
                            dlt = j - qc * 4
                            p.op("V", lambda e, j=j, dlt=dlt, sb_pT=sb_pT: e.tensor_tensor(
                                sb_pT[j % 8][:, :], sb_pT[j % 8][:, :],
                                sb_mask[:, dlt * CH:(dlt + 1) * CH], ALU.mult),
                                (f"pT{j % 8}", "mask"), (f"pT{j % 8}",), tag="msk")

                    for j in range(min(DEPTH, jmax)):
                        emit_S_mm(j)
                        emit_exp(j)
                    for j in range(jmax):
                        if j == 2:
                            flush_st()
                        if j + DEPTH < jmax:
                            emit_S_mm(j + DEPTH)
                        last = j == jmax - 1
                        p.op("T", lambda e, hs=hs, j=j, last=last, opar=opar, sb_V=sb_V, sb_pT=sb_pT, ps_O=ps_O: e.matmul(
                            ps_O[opar][:, :], sb_V[hs][:, j * 128:(j + 1) * 128], sb_pT[j % 8][:, :],
                            start=(j == 0), stop=last),
                            (f"Vh{hs}", f"pT{j % 8}"), (f"psO{opar}",), tag="AV")
                        p.op("T", lambda e, j=j, last=last, sb_pT=sb_pT, ps_L=ps_L: e.matmul(
                            ps_L[0:1, :], sb_onec[:, :], sb_pT[j % 8][:, :],
                            start=(j == 0), stop=last),
                            ("onec", f"pT{j % 8}"), ("psL",), tag="Lmm")
                        if j + DEPTH < jmax:
                            emit_exp(j + DEPTH)
                    p.op("V", lambda e, qs=qs, sb_rc=sb_rc, ps_L=ps_L: e.reciprocal(
                        sb_rc[qs][:, :], ps_L[0:1, :]), ("psL",), (f"rc{qs}",), tag="rcp")
                    p.op("V", lambda e, qs=qs, opar=opar, sb_ao=sb_ao, ps_O=ps_O: e.tensor_copy(
                        sb_ao[qs][:, :], ps_O[opar][:, :]),
                        (f"psO{opar}",), (f"ao{qs}",), tag="aoev")
                    def st(h=h, qc=qc, qs=qs):
                        p.dma("S", lambda e, h=h, qc=qc, qs=qs, sb_rc=sb_rc: e.dma_start(
                            s_rcs[h:h + 1, qc * CH:(qc + 1) * CH], sb_rc[qs][:, :]),
                            (f"rc{qs}",), ("rcsh",), f"d_rcst{qs}")
                        p.dma("S", lambda e, h=h, qc=qc, qs=qs, sb_ao=sb_ao: e.dma_start(
                            s_aout[h * 128:(h + 1) * 128, qc * CH:(qc + 1) * CH], sb_ao[qs][:, :]),
                            (f"ao{qs}",), ("aout",), f"d_aost{qs}")
                    pend_st.append(st)
            flush_st()
            p.barrier()

        def phase_D_inner(base, sb_wo, casts=()):
            for cfn, ckey in casts:
                p.dma("G", cfn, (), (ckey,), "d_wcast")
            ard = Arena(base)
            sb_aT = [ard.al(f"sb_aT{i}", [128, NKT * CH], BF16) for i in range(2)]
            sb_oh = ard.al("sb_oh", [16, NKT * 128], BF16)
            sb_rc16 = [ard.al(f"sb_rc16{i}", [16, CH], F32) for i in range(2)]
            sb_rc16b = [ard.al(f"sb_rc16b{i}", [16, CH], BF16) for i in range(2)]
            ps_B2 = [PS[2], PS[3]]
            p.dma("G", lambda e: e.dma_start(sb_oh[:, :], d_oh), (), ("oh",), "d_oh")
            sb_xr = [ard.al(f"sb_xr{i}", [128, D], F32) for i in range(2)]
            sb_z = ard.al("sb_z", [128, D], F32)
            sb_t = ard.al("sb_t", [128, D], F32)
            sb_st = ard.al("sb_st", [128, 8], F32)
            sb_xtf = [ard.al(f"sb_xtf{i}", [128, D], F32) for i in range(2)]
            sb_xtb = [ard.al(f"sb_xtb{i}", [128, D], BF16) for i in range(2)]
            ps_a = [PS[0], PS[1]]

            for c in range(T // CH):
                sl = c % 2
                fns = [
                    (lambda kt: (lambda e, c=c, sl=sl, sb_aT=sb_aT: e.dma_start(
                        sb_aT[sl][:, kt * CH:(kt + 1) * CH],
                        s_aout[kt * 128:(kt + 1) * 128, c * CH:(c + 1) * CH])))(kt)
                    for kt in range(NKT)]
                p.dma_group("Y", fns, ("aout",), f"aT{sl}", f"d_aT{sl}")
                p.dma("Y", lambda e, c=c, sl=sl, sb_rc16=sb_rc16: e.dma_start(
                    sb_rc16[sl][:, :], s_rcs[:, c * CH:(c + 1) * CH]),
                    ("rcsh",), (f"rc16_{sl}",), f"d_rc16_{sl}")
                p.op("S", lambda e, sl=sl, sb_rc16=sb_rc16, sb_rc16b=sb_rc16b: e.copy(
                    sb_rc16b[sl][:, :], sb_rc16[sl][:, :]),
                    (f"rc16_{sl}",), (f"rc16b{sl}",))
                for kt in range(NKT):
                    bp = kt % 2
                    p.op("T", lambda e, kt=kt, sl=sl, bp=bp, sb_oh=sb_oh, sb_rc16b=sb_rc16b, ps_B2=ps_B2: e.matmul(
                        ps_B2[bp][:, :], sb_oh[:, kt * 128:(kt + 1) * 128],
                        sb_rc16b[sl][:, :], start=True, stop=True),
                        ("oh", f"rc16b{sl}"), (f"psB2{bp}",), tag="Bmm2")
                    p.op("V", lambda e, kt=kt, sl=sl, bp=bp, sb_aT=sb_aT, ps_B2=ps_B2: e.tensor_tensor(
                        sb_aT[sl][:, kt * CH:(kt + 1) * CH],
                        sb_aT[sl][:, kt * CH:(kt + 1) * CH], ps_B2[bp][:, :], ALU.mult),
                        (f"aT{sl}", f"psB2{bp}"), (f"aT{sl}",), tag="ascl")
                for mt in range(4):
                    msl = mt % 2
                    tt = c * 4 + mt
                    p.dma("Y", lambda e, tt=tt, msl=msl, sb_xr=sb_xr: e.dma_start(
                        sb_xr[msl][:, :], d_x[tt * 128:(tt + 1) * 128, :]),
                        (), (f"xr{msl}",), f"d_xr{msl}")
                    for nchk in range(4):
                        pp = nchk % 2
                        def mmd(e, sl=sl, mt=mt, nchk=nchk, pp=pp, sb_aT=sb_aT, sb_wo=sb_wo, ps_a=ps_a):
                            for kt in range(NKT):
                                ins = e.matmul(ps_a[pp][:, :],
                                               sb_aT[sl][:, kt * CH + mt * 128:kt * CH + (mt + 1) * 128],
                                               sb_wo[:, kt * D + nchk * CH:kt * D + (nchk + 1) * CH],
                                               start=(kt == 0), stop=(kt == NKT - 1))
                            return ins
                        p.op("T", mmd, (f"aT{sl}", "wo"), (f"psa{pp}",), tag="dmm")
                        p.op("V", lambda e, msl=msl, nchk=nchk, pp=pp, sb_z=sb_z, sb_xr=sb_xr, ps_a=ps_a: e.tensor_tensor(
                            sb_z[:, nchk * CH:(nchk + 1) * CH], ps_a[pp][:, :],
                            sb_xr[msl][:, nchk * CH:(nchk + 1) * CH], ALU.add),
                            (f"psa{pp}", f"xr{msl}"), ("z",), tag="res")
                    layer_norm("z", sb_z[:, :], f"xtf{msl}", sb_xtf[msl][:, :], "d1", sb_t, sb_st)
                    p.op("V", lambda e, msl=msl, sb_xtb=sb_xtb, sb_xtf=sb_xtf: e.tensor_copy(
                        sb_xtb[msl][:, :], sb_xtf[msl][:, :]),
                        (f"xtf{msl}",), (f"xtb{msl}",))
                    p.dma("G", lambda e, tt=tt, msl=msl, sb_xtf=sb_xtf: e.dma_start(
                        s_xtf[tt * 128:(tt + 1) * 128, :], sb_xtf[msl][:, :]),
                        (f"xtf{msl}",), ("xtfh",), f"d_xtfst{msl}")
                    p.dma("G", lambda e, tt=tt, msl=msl, sb_xtb=sb_xtb: e.dma_start(
                        s_xtbf[tt * 128:(tt + 1) * 128, :], sb_xtb[msl][:, :]),
                        (f"xtb{msl}",), ("xtbfh",), f"d_xtbst{msl}")
            p.barrier()

        def phase_E0():
            ar = Arena(MARK_SMALL)
            sb_xtT = [ar.al(f"sb_xtTg{i}", [128, NKT * CH], BF16) for i in range(2)]
            sb_sc = ar.al("sb_sc", [128, NE], F32)
            sb_t8 = ar.al("sb_t8", [128, NE], F32)
            sb_i1 = ar.al("sb_i1", [128, NE], F32)
            sb_i2 = ar.al("sb_i2", [128, NE], F32)
            sb_r8 = ar.al("sb_r8", [128, 2], F32)
            sb_bias = ar.al("sb_bias", [128, NE], F32)
            ps_g = [PS[0], PS[1]]
            ps_bb = PS[2]

            p.op("T", lambda e, ps_bb=ps_bb: e.matmul(ps_bb[:, 0:NE], sb_oner[:, :], sb_eb[:, :],
                                                      start=True, stop=True),
                 ("oner", "ebias"), ("psbb",))
            p.op("V", lambda e, sb_bias=sb_bias, ps_bb=ps_bb: e.tensor_copy(
                sb_bias[:, :], ps_bb[:, 0:NE]), ("psbb",), ("bias",))

            for c in range(T // CH):
                sl = c % 2
                load_xT(sb_xtT[sl], f"xtTg{sl}", s_xtbf, c, ("xtbfh",))
                for mt in range(4):
                    tt = c * 4 + mt
                    pp = mt % 2
                    def mmg(e, sl=sl, mt=mt, pp=pp, sb_xtT=sb_xtT, ps_g=ps_g):
                        for kt in range(NKT):
                            ins = e.matmul(ps_g[pp][:, 0:NE],
                                           sb_xtT[sl][:, kt * CH + mt * 128:kt * CH + (mt + 1) * 128],
                                           sb_gate[:, kt * NE:(kt + 1) * NE],
                                           start=(kt == 0), stop=(kt == NKT - 1))
                        return ins
                    p.op("T", mmg, (f"xtTg{sl}", "gate"), (f"psg{pp}",), tag="gmm")
                    p.op("S", lambda e, pp=pp, sb_sc=sb_sc, ps_g=ps_g: e.activation(
                        sb_sc[:, :], ps_g[pp][:, 0:NE], AF.Sigmoid), (f"psg{pp}",), ("sc",))
                    p.op("V", lambda e, sb_t8=sb_t8, sb_sc=sb_sc, sb_bias=sb_bias: e.tensor_tensor(
                        sb_t8[:, :], sb_sc[:, :], sb_bias[:, :], ALU.add), ("sc", "bias"), ("t8",))
                    p.op("V", lambda e, sb_r8=sb_r8, sb_t8=sb_t8: e.reduce_max(
                        sb_r8[:, 0:1], sb_t8[:, :], axis=AX.X), ("t8",), ("r80",))
                    p.op("V", lambda e, sb_i1=sb_i1, sb_t8=sb_t8, sb_r8=sb_r8: e.tensor_scalar(
                        sb_i1[:, :], sb_t8[:, :], sb_r8[:, 0:1], None, ALU.is_ge),
                        ("t8", "r80"), ("i1",))
                    p.op("V", lambda e, sb_i2=sb_i2, sb_i1=sb_i1, sb_t8=sb_t8: e.scalar_tensor_tensor(
                        sb_i2[:, :], sb_i1[:, :], -1e9, sb_t8[:, :], ALU.mult, ALU.add),
                        ("i1", "t8"), ("i2t",))
                    p.op("V", lambda e, sb_r8=sb_r8, sb_i2=sb_i2: e.reduce_max(
                        sb_r8[:, 1:2], sb_i2[:, :], axis=AX.X), ("i2t",), ("r81",))
                    p.op("V", lambda e, sb_i2=sb_i2, sb_r8=sb_r8: e.tensor_scalar(
                        sb_i2[:, :], sb_i2[:, :], sb_r8[:, 1:2], None, ALU.is_ge),
                        ("i2t", "r81"), ("i2",))
                    p.op("V", lambda e, sb_i1=sb_i1, sb_i2=sb_i2: e.tensor_tensor(
                        sb_i1[:, :], sb_i1[:, :], sb_i2[:, :], ALU.add), ("i1", "i2"), ("i1b",))
                    p.op("V", lambda e, tt=tt, sb_sc=sb_sc, sb_i1=sb_i1: e.tensor_tensor(
                        sb_mall[:, tt * NE:(tt + 1) * NE], sb_sc[:, :], sb_i1[:, :], ALU.mult),
                        ("sc", "i1b"), ("mall",))
            p.barrier()

        def phase_E():
            ar = Arena(MARK_SMALL)
            sb_xtT = [ar.al(f"sb_xtT{i}", [128, NKT * CH], BF16) for i in range(2)]
            sb_xm = [ar.al(f"sb_xm{i}", [128, NKT * CH], BF16) for i in range(2)]
            sb_w1p = [ar.al(f"sb_w1p{i}", [128, NKT * 128], BF16) for i in range(2)]
            sb_w3p = [ar.al(f"sb_w3p{i}", [128, NKT * 128], BF16) for i in range(2)]
            sb_w2p = ar.al("sb_w2p", [128, NJ * D], BF16)
            sb_hT = ar.al("sb_hT", [128, NJ * CH], BF16)
            sb_ss = [ar.al(f"sb_ss{i}", [128, CH], BF16) for i in range(2)]
            sb_mr = [ar.al(f"sb_mr{i}", [1, CH], BF16) for i in range(2)]
            sb_nB = [ar.al(f"sb_nBe{i}", [128, CH], BF16) for i in range(2)]
            sb_ya = ar.al("sb_ya", [128, 4 * D], F32)
            sb_t = ar.al("sb_te", [128, D], F32)
            sb_st = ar.al("sb_ste", [128, 8], F32)
            sb_o = [ar.al(f"sb_o{i}", [128, D], F32) for i in range(2)]
            ps_u1 = [PS[0], PS[1]]
            ps_u3 = [PS[2], PS[3]]
            ps_y = [PS[4], PS[5]]
            ps_mr = PS[6]
            ps_B = PS[7]

            def prep_mask(mc, ei):
                ee = ei - 1
                par = ei % 2
                sl = mc % 2
                def trg(e, mc=mc, ee=ee, ps_mr=ps_mr):
                    for tt4 in range(4):
                        ins = e.transpose(ps_mr[0:1, tt4 * 128:(tt4 + 1) * 128],
                                          sb_mall[:, (mc * 4 + tt4) * NE + ee:
                                                  (mc * 4 + tt4) * NE + ee + 1],
                                          sb_idf[:, :])
                    return ins
                p.op("T", trg, ("mall", "idf"), ("psmr",), tag="mtr")
                p.op("S", lambda e, par=par, sb_mr=sb_mr, ps_mr=ps_mr: e.copy(
                    sb_mr[par][:, :], ps_mr[0:1, :]), ("psmr",), (f"mr{par}",))
                p.op("T", lambda e, par=par, sb_mr=sb_mr, ps_B=ps_B: e.matmul(
                    ps_B[:, :], sb_oner[:, :], sb_mr[par][:, :], start=True, stop=True),
                    ("oner", f"mr{par}"), ("psBe",), tag="Bemm")
                p.op("S", lambda e, par=par, sb_nB=sb_nB, ps_B=ps_B: e.copy(
                    sb_nB[par][:, :], ps_B[:, :]), ("psBe",), (f"nBe{par}",))
                for kt in range(NKT):
                    p.op("V", lambda e, kt=kt, sl=sl, par=par, sb_xm=sb_xm, sb_xtT=sb_xtT, sb_nB=sb_nB: e.tensor_tensor(
                        sb_xm[par][:, kt * CH:(kt + 1) * CH],
                        sb_xtT[sl][:, kt * CH:(kt + 1) * CH], sb_nB[par][:, :], ALU.mult),
                        (f"xtT{sl}", f"nBe{par}"), (f"xm{par}",), tag="xm")

            for mc in range(T // CH):
                sl = mc % 2
                load_xT(sb_xtT[sl], f"xtT{sl}", s_xtbf, mc, ("xtbfh",))
                fns = [
                    (lambda tt4: (lambda e, mc=mc, sb_ya=sb_ya: e.dma_start(
                        sb_ya[:, tt4 * D:(tt4 + 1) * D],
                        s_xtf[(mc * 4 + tt4) * 128:(mc * 4 + tt4 + 1) * 128, :])))(tt4)
                    for tt4 in range(4)]
                p.dma_group("Y", fns, ("xtfh",), "ya", "d_ya")
                for ei in range(NE + 1):
                    routed = ei > 0
                    xsl = ei % 2
                    xkey = f"xm{xsl}" if routed else f"xtT{sl}"
                    xsrc = sb_xm[xsl] if routed else sb_xtT[sl]
                    if routed:
                        ee = ei - 1
                        w1src, w3src, w2src = b_w1[ee], b_w3[ee], b_w2[ee]
                        w1key, w3key, w2key = f"w1b{ee}", f"w3b{ee}", f"w2b{ee}"
                    else:
                        w1src, w3src, w2src = b_sw1, b_sw3, b_sw2
                        w1key, w3key, w2key = "sw1b", "sw3b", "sw2b"
                    for j in (0, 1):
                        p.dma("G", lambda e, j=j, w1src=w1src, sb_w1p=sb_w1p: e.dma_start(
                            sb_w1p[j % 2][:, :], w1src[j]),
                            (w1key,), (f"w1p{j % 2}",), f"d_w1p{j % 2}")
                        p.dma("G", lambda e, j=j, w3src=w3src, sb_w3p=sb_w3p: e.dma_start(
                            sb_w3p[j % 2][:, :], w3src[j]),
                            (w3key,), (f"w3p{j % 2}",), f"d_w3p{j % 2}")
                    fns = [
                        (lambda j: (lambda e, w2src=w2src, sb_w2p=sb_w2p: e.dma_start(
                            sb_w2p[:, j * D:(j + 1) * D], w2src[j])))(j)
                        for j in range(NJ)]
                    p.dma_group("G", fns, (w2key,), "w2p", "d_w2p")
                    for j in range(NJ):
                        ws = j % 2
                        def mmu1(e, ws=ws, xsrc=xsrc, sb_w1p=sb_w1p, ps_u1=ps_u1):
                            for kt in range(NKT):
                                ins = e.matmul(ps_u1[ws][:, :],
                                               sb_w1p[ws][:, kt * 128:(kt + 1) * 128],
                                               xsrc[:, kt * CH:(kt + 1) * CH],
                                               start=(kt == 0), stop=(kt == NKT - 1))
                            return ins
                        p.op("T", mmu1, (f"w1p{ws}", xkey), (f"psu1{ws}",), tag="u1")
                        def mmu3(e, ws=ws, xsrc=xsrc, sb_w3p=sb_w3p, ps_u3=ps_u3):
                            for kt in range(NKT):
                                ins = e.matmul(ps_u3[ws][:, :],
                                               sb_w3p[ws][:, kt * 128:(kt + 1) * 128],
                                               xsrc[:, kt * CH:(kt + 1) * CH],
                                               start=(kt == 0), stop=(kt == NKT - 1))
                            return ins
                        p.op("T", mmu3, (f"w3p{ws}", xkey), (f"psu3{ws}",), tag="u3")
                        if j + 2 < NJ:
                            p.dma("G", lambda e, j=j, w1src=w1src, sb_w1p=sb_w1p: e.dma_start(
                                sb_w1p[j % 2][:, :], w1src[j + 2]),
                                (w1key,), (f"w1p{j % 2}",), f"d_w1p{j % 2}")
                            p.dma("G", lambda e, j=j, w3src=w3src, sb_w3p=sb_w3p: e.dma_start(
                                sb_w3p[j % 2][:, :], w3src[j + 2]),
                                (w3key,), (f"w3p{j % 2}",), f"d_w3p{j % 2}")
                        p.op("S", lambda e, ws=ws, sb_ss=sb_ss, ps_u1=ps_u1: e.activation(
                            sb_ss[ws][:, :], ps_u1[ws][:, :], AF.Silu),
                            (f"psu1{ws}",), (f"ss{ws}",), tag="silu")
                        p.op("V", lambda e, j=j, ws=ws, sb_hT=sb_hT, sb_ss=sb_ss, ps_u3=ps_u3: e.tensor_tensor(
                            sb_hT[:, j * CH:(j + 1) * CH], sb_ss[ws][:, :], ps_u3[ws][:, :], ALU.mult),
                            (f"ss{ws}", f"psu3{ws}"), (f"hT{j}",), tag="h")
                    if ei < NE:
                        prep_mask(mc, ei + 1)
                    for tt4 in range(4):
                        for nchk in range(4):
                            pp = (tt4 * 4 + nchk) % 2
                            def mmy(e, tt4=tt4, nchk=nchk, pp=pp, sb_hT=sb_hT, sb_w2p=sb_w2p, ps_y=ps_y):
                                for j in range(NJ):
                                    ins = e.matmul(ps_y[pp][:, :],
                                                   sb_hT[:, j * CH + tt4 * 128:j * CH + (tt4 + 1) * 128],
                                                   sb_w2p[:, j * D + nchk * CH:j * D + (nchk + 1) * CH],
                                                   start=(j == 0), stop=(j == NJ - 1))
                                return ins
                            p.op("T", mmy, tuple(f"hT{j}" for j in range(NJ)) + ("w2p",),
                                 (f"psy{pp}",), tag="ymm")
                            p.op("V", lambda e, tt4=tt4, nchk=nchk, pp=pp, sb_ya=sb_ya, ps_y=ps_y: e.tensor_tensor(
                                sb_ya[:, tt4 * D + nchk * CH:tt4 * D + (nchk + 1) * CH],
                                sb_ya[:, tt4 * D + nchk * CH:tt4 * D + (nchk + 1) * CH],
                                ps_y[pp][:, :], ALU.add),
                                (f"psy{pp}", "ya"), ("ya",), tag="yacc")
                for tt4 in range(4):
                    tt = mc * 4 + tt4
                    osl = tt4 % 2
                    layer_norm("ya", sb_ya[:, tt4 * D:(tt4 + 1) * D], f"o{osl}",
                               sb_o[osl][:, :], "e2", sb_t, sb_st)
                    p.dma("S", lambda e, tt=tt, osl=osl, sb_o=sb_o: e.dma_start(
                        d_out[tt * 128:(tt + 1) * 128, :], sb_o[osl][:, :]),
                        (f"o{osl}",), ("outh",), f"d_ost{osl}")
            p.barrier()

        if "B1" in _PHASES:
            phase_B1()
        if "B2" in _PHASES:
            phase_B2()
        if "B3" in _PHASES:
            phase_B3()
        if "C" in _PHASES or "D" in _PHASES:
            phase_CD()
        if "E0" in _PHASES:
            phase_E0()
        if "E" in _PHASES:
            phase_E()

        with nc.Block() as b0:
            def clr(g):
                nums = sorted(h.num for h in p.handles.values())
                if nums:
                    g.dma_reset(range(min(nums), max(nums) + 1))
                    g.sem_clear(range(min(nums), max(nums) + 1))
            b0.gpsimd(clr)
        nc.all_engine_barrier()
        with nc.Block() as block:
            p.replay(block)

    ninstr = sum(len(v) for v in p.ops.values())
    build_program.last_p = p
    return nc, ninstr


def kernel(L0_attn_x, L0_attn_wq, L0_attn_w_down, L0_attn_w_up, L0_attn_wo,
           L0_ffn_gate_w, L0_ffn_w1, L0_ffn_w2, L0_ffn_w3,
           L0_ffn_sw1, L0_ffn_sw2, L0_ffn_sw3, L0_ffn_expert_bias):
    x = np.ascontiguousarray(np.asarray(L0_attn_x, dtype=np.float32))
    wq = np.asarray(L0_attn_wq, dtype=np.float32)
    w_down = np.asarray(L0_attn_w_down, dtype=np.float32)
    w_up = np.asarray(L0_attn_w_up, dtype=np.float32)
    wo = np.asarray(L0_attn_wo, dtype=np.float32)
    gate_w = np.asarray(L0_ffn_gate_w, dtype=np.float32)
    w1 = np.asarray(L0_ffn_w1, dtype=np.float32)
    w2 = np.asarray(L0_ffn_w2, dtype=np.float32)
    w3 = np.asarray(L0_ffn_w3, dtype=np.float32)
    sw1 = np.asarray(L0_ffn_sw1, dtype=np.float32)
    sw2 = np.asarray(L0_ffn_sw2, dtype=np.float32)
    sw3 = np.asarray(L0_ffn_sw3, dtype=np.float32)
    ebias = np.asarray(L0_ffn_expert_bias, dtype=np.float32)

    perm = _rope_perm()
    w_up_h = w_up.reshape(LAT, H, 2 * DH)
    w_up_k = np.ascontiguousarray(w_up_h[:, :, :DH].reshape(LAT, D))[:, perm]
    w_up_v = np.ascontiguousarray(w_up_h[:, :, DH:].reshape(LAT, D))

    inv_freq = 10000.0 ** (-(np.arange(64, dtype=np.float64)) / 64.0)
    ang = np.arange(T, dtype=np.float64)[:, None] * inv_freq[None, :]
    kk = np.arange(128)[:, None]
    qq = np.arange(CH)[None, :]

    shared = {
        "wq_r": _rhs_tiles(wq[:, perm]), "wdn_l": _lhsT_tiles(w_down),
        "wupk_r": _rhs_tiles(w_up_k), "wupv_r": _rhs_tiles(w_up_v),
        "wo_r": _rhs_tiles(wo), "gate_r": _rhs_tiles(np.ascontiguousarray(gate_w.T)),
        "w1p": np.stack([_panel_lhsT(np.ascontiguousarray(w1[e].T)) for e in range(NE)]),
        "w3p": np.stack([_panel_lhsT(np.ascontiguousarray(w3[e].T)) for e in range(NE)]),
        "w2p": np.stack([_panel_rhs(np.ascontiguousarray(w2[e].T)) for e in range(NE)]),
        "sw1p": _panel_lhsT(np.ascontiguousarray(sw1.T)),
        "sw3p": _panel_lhsT(np.ascontiguousarray(sw3.T)),
        "sw2p": _panel_rhs(np.ascontiguousarray(sw2.T)),
        "ebias": np.ascontiguousarray(ebias.reshape(1, NE)),
        "cosb": np.ascontiguousarray(
            np.cos(ang).astype(np.float32).reshape(T // 128, 128, 64).transpose(1, 0, 2)),
        "sinb": np.ascontiguousarray(
            np.sin(ang).astype(np.float32).reshape(T // 128, 128, 64).transpose(1, 0, 2)),
        "masks": np.ascontiguousarray(
            np.stack([(kk <= qq - 128 * dlt).astype(np.float32) for dlt in range(4)])),
        "onec": np.ones((128, 1), np.float32),
        "onehot": np.ascontiguousarray(
            np.repeat(np.eye(16, dtype=np.float32), 128, axis=1)),
        "oner": np.ones((1, 128), np.float32),
        "identf": np.ascontiguousarray(np.eye(128, dtype=np.float32)),
    }
    in_maps = [dict(shared, x=np.ascontiguousarray(x[b])) for b in range(8)]

    nc, _ = build_program()
    res = run_bass_kernel_spmd(nc, in_maps, core_ids=list(range(8)))
    return np.stack([np.asarray(r["out"], dtype=np.float32) for r in res.results])


if __name__ == "__main__":
    import time
    t0 = time.time()
    nc, n = build_program()
    print(f"built ok: {n} stream entries in {time.time() - t0:.1f}s")
